# version 06
# baseline (speedup 1.0000x reference)
import math
import numpy as np
from contextlib import ExitStack

import concourse.bass as bass
import concourse.mybir as mybir
from concourse.bass_utils import run_bass_kernel_spmd

F32 = mybir.dt.float32
BF16 = mybir.dt.bfloat16
AF = mybir.ActivationFunctionType
ALU = mybir.AluOpType
AX = mybir.AxisListType

NCORES = 8
S = 2048
D = 1024
HD = 64
NPAIR = 8
FF = 4096
E = 8
CAP = 10240.0
QSC = 1.0 / math.sqrt(math.sqrt(float(HD)))
NDT = D // 128
NFT = FF // 128
NTOKM = S // 128
NCH = S // 512


class Sem:
    def __init__(self, h):
        self.h = h
        self.n = 0

    def inc(self, k=1):
        self.n += k
        return self.n


class Builder:
    def __init__(self, nb=4, nfb=28, taps=()):
        self.nb = nb
        self.nfb = nfb
        self.nc = bass.Bass("TRN2", target_bir_lowering=False)
        self.q = {k: [] for k in ("t", "v", "s", "g", "y")}
        self.taps = {}
        self.es = ExitStack()
        self.es.__enter__()
        self._declare_dram(taps)
        self._declare_sbuf()
        self._declare_sems()
        self.o_prev_v = 0
        self.m_y_mm_end = 0
        self.m_h_mm_end = 0
        self.xtb_guard = [0, 0]
        self.xtb_guard_s = [0, 0]
        self.xtb_guard_v = [0, 0]
        self.o_free_st = [0, 0]
        self.r_mm_end = 0

    def _declare_dram(self, taps):
        nc = self.nc
        i = {}

        def din(name, shape, dtype=F32):
            i[name] = nc.dram_tensor(name, shape, dtype, kind="ExternalInput").ap()

        din("x_own", [self.nb, S, D])
        din("xT_own", [self.nb, D, S])
        if self.nfb:
            din("xT_for", [self.nfb, D, S])
        din("m28", [1, max(self.nfb, 1)])
        for w in ("wq", "wk", "wv", "wo"):
            din(w, [D, D])
        for b in ("bq", "bk", "bv", "bo"):
            din(b, [1, D])
        din("Wr", [D, E])
        din("W1", [E, D, FF])
        din("b1", [E, FF])
        din("W2", [E, FF, D])
        din("b2", [E, D])
        din("cosrep", [128, S])
        din("sinrep", [128, S])
        din("ltri_b", [128, 128], BF16)
        din("ltri_f", [128, 128])
        self.i = i
        self.y_out = nc.dram_tensor("y_out", [self.nb, S, D], F32,
                                    kind="ExternalOutput").ap()
        scr = {}
        for n in ("wq1s", "wq2s", "wk1s", "wk2s", "wos"):
            scr[n] = nc.dram_tensor(n, [D, D], BF16, kind="Internal").ap()
        scr["W1s"] = nc.dram_tensor("W1s", [E, D, FF], BF16, kind="Internal").ap()
        scr["W2s"] = nc.dram_tensor("W2s", [E, FF, D], BF16, kind="Internal").ap()
        self.scr = scr
        for name, shape, dtyp in taps:
            self.taps[name] = nc.dram_tensor(name, shape, dtyp,
                                             kind="ExternalOutput").ap()

    def _declare_sbuf(self):
        nc = self.nc

        def sb(name, shape, dtype):
            t = self.es.enter_context(nc.sbuf_tensor(name, shape, dtype))
            setattr(self, name, t)

        sb("xnT", [128, NDT * S], BF16)
        sb("sumT", [128, NTOKM * D], BF16)
        sb("AH", [128, NDT * S], BF16)
        sb("cosr", [128, S], BF16)
        sb("sinr", [128, S], BF16)
        sb("ONESB", [1, 512], BF16)
        sb("ONECB", [128, 1], BF16)
        sb("ONECF", [128, 1], F32)
        sb("ltriB", [128, 128], BF16)
        sb("ltriF", [128, 128], F32)
        self.ONESF = self.ltriF
        sb("selq", [128, 2], BF16)
        sb("wr_f", [128, NDT * E], F32)
        sb("wr_b", [128, NDT * E], BF16)
        sb("wr_lo", [128, NDT * E], BF16)
        sb("negcs_f", [1, E], F32)
        sb("negcs_b", [1, E], BF16)
        sb("negcsb_t", [128, E], F32)
        sb("m28_sb", [1, max(self.nfb, 1)], F32)
        sb("bq1r", [1, D], BF16)
        sb("bq2r", [1, D], BF16)
        sb("bk1r", [1, D], BF16)
        sb("bk2r", [1, D], BF16)
        sb("bo_r", [1, D], BF16)
        sb("bv_r", [1, D], BF16)
        sb("b2_r", [1, D], BF16)
        sb("b1c", [128, E * 32], F32)
        sb("rank_sb", [128, NTOKM * 16], F32)
        sb("oh1_sb", [128, NTOKM * E], BF16)
        sb("oh2_sb", [128, NTOKM * E], BF16)
        sb("w1c", [128, NTOKM], F32)
        sb("w2c", [128, NTOKM], F32)
        sb("w8_sb", [128, NTOKM * E], F32)
        sb("runoff", [1, 16], F32)
        sb("baseoff", [1, 16], F32)
        sb("offb", [128, 16], F32)
        sb("ohacc", [128, 16], F32)
        sb("keep_t", [128, 16], F32)
        sb("rstd_cols", [128, NTOKM], F32)
        sb("smalls", [128, 16], F32)
        sb("negu_r", [1, 8], BF16)
        sb("xtb", [128, NDT * 512], BF16)
        sb("xt32", [128, NDT * 128], F32)
        sb("xlo", [128, NDT * 128], BF16)
        sb("xsq", [128, 512], BF16)
        sb("sxrow", [1, 512], F32)
        sb("mrow_t", [1, 512], F32)
        sb("var_t", [1, 512], F32)
        sb("rstd_f", [1, 512], F32)
        sb("rstd_bf", [1, S], BF16)
        sb("mrstd_bf", [1, S], BF16)
        sb("tmpa", [128, 512], F32)
        sb("s_sb", [128, E], F32)
        sb("top_sb", [128, E], F32)
        sb("m1_sb", [128, E], F32)
        sb("c1_sb", [128, E], F32)
        sb("zer8", [128, E], F32)
        sb("colA", [128, 1], F32)
        sb("colB", [128, 1], F32)
        sb("mrow_b", [1, 512], BF16)
        sb("qTp", [128, S], BF16)
        sb("kTp", [128, S], BF16)
        sb("v_sb", [128, NTOKM * 130], BF16)
        sb("wqp", [128, 2 * D], BF16)
        sb("wvp", [128, D], BF16)
        sb("att", [128, 4096], BF16)
        sb("aun", [128, 512], BF16)
        sb("wop", [128, D], BF16)
        sb("w1blk", [128, NDT * 256], BF16)
        sb("w1blk2", [128, NDT * 256], BF16)
        sb("w2blk", [128, D], BF16)
        sb("w2blk2", [128, D], BF16)
        sb("xo_t", [128, D], F32)
        sb("xo_t2", [128, D], F32)
        self.stage = self.tmpa
        self.stage_b = self.xsq
        self.rowstg = self.xt32
        self.zrec = self.rstd_f
        self.zrec_b = self.mrow_b
        self.PS = self.es.enter_context(nc.psum_tensor("PS", [128, 4096], F32))

    def _declare_sems(self):
        names = ("d_wa d_wo d_xt0 d_xt1 d_w10 d_w11 d_w20 d_w21 d_xo0 d_xo1 "
                 "d_st0 d_st1 d_msc d_x32 d_wv d_cos d_b2 d_wos d_tap "
                 "c_v c_s c_g c_mm c_ev").split()
        names += ["d_cv%d" % e for e in range(E)]
        for n in names:
            setattr(self, n, Sem(self.es.enter_context(self.nc.semaphore(n))))
        self.d_xt = [self.d_xt0, self.d_xt1]
        self.d_w1 = [self.d_w10, self.d_w11]
        self.d_w2 = [self.d_w20, self.d_w21]
        self.d_xo = [self.d_xo0, self.d_xo1]
        self.d_st = [self.d_st0, self.d_st1]
        self.d_cv = [getattr(self, "d_cv%d" % e) for e in range(E)]

    def em(self, eng, f):
        self.q[eng].append(f)

    def wait(self, eng, sem, n=None):
        v = sem.n if n is None else n
        if v > 0:
            self.em(eng, lambda e, h=sem.h, k=v: e.wait_ge(h, k))

    def dma(self, eng, out, in_, sem):
        self.em(eng, lambda e, o=out, i=in_, h=sem.h:
                e.dma_start(out=o, in_=i).then_inc(h, 16))
        return sem.inc(16)

    def dmaS(self, eng, out, in_, sem):
        n = self.dma(eng, out, in_, sem)
        self.wait(eng, sem, n)
        return n

    def mm(self, out, lhsT, rhs, start, stop, sem=None, tp=None):
        if sem is None:
            self.em("t", lambda e, o=out, l=lhsT, r=rhs, a=start, b=stop, t=tp:
                    e.matmul(o, l, r, start=a, stop=b, tile_position=t))
            return None
        self.em("t", lambda e, o=out, l=lhsT, r=rhs, a=start, b=stop, t=tp, h=sem.h:
                e.matmul(o, l, r, start=a, stop=b, tile_position=t).then_inc(h, 1))
        return sem.inc()

    def op(self, eng, f, sem=None):
        if sem is None:
            self.em(eng, lambda e, g=f: g(e))
            return None
        self.em(eng, lambda e, g=f, h=sem.h: g(e).then_inc(h, 1))
        return sem.inc()

    @staticmethod
    def ap(t, off, dims):
        a = t if isinstance(t, bass.AP) else t[:]
        return bass.AP(a.tensor, a.offset + off, dims)

    def phase_W(self):
        PS = self.PS
        i = self.i
        for f in (
            lambda e: e.memset(self.ONESB[:], 1.0),
            lambda e: e.memset(self.ONECB[:], 1.0),
            lambda e: e.memset(self.ONECF[:], 1.0),
            lambda e: e.memset(self.zer8[:], 0.0),
            lambda e: e.memset(self.runoff[:], 0.0),
            lambda e: e.memset(self.baseoff[:], 0.0),
            lambda e: e.memset(self.ohacc[:], 0.0),
            lambda e: e.memset(self.offb[:], 0.0),
            lambda e: e.memset(self.smalls[:, 6:7], 1e-5),
            lambda e: e.memset(self.selq[:], 0.0),
            lambda e: e.memset(self.selq[0:64, 0:1], 1.0),
        ):
            self.op("g", f)
        self.op("g", lambda e: e.memset(self.selq[64:128, 1:2], 1.0), self.c_g)
        g_const = self.c_g.n

        self.dmaS("y", self.ltriB[:], i["ltri_b"], self.d_msc)
        self.dmaS("y", self.ltriF[:], i["ltri_f"], self.d_msc)
        self.dmaS("y", self.wr_f[:],
                  self.ap(i["Wr"], 0, [[E, 128], [128 * E, NDT], [1, E]]), self.d_msc)
        if self.nfb:
            self.dmaS("y", self.m28_sb[:], i["m28"], self.d_msc)
        self.dma("g", self.cosr[:], i["cosrep"], self.d_cos)
        self.dma("g", self.sinr[:], i["sinrep"], self.d_cos)
        self.d_cos_n = self.d_cos.n
        d_consts = self.d_msc.n

        self.dma("g", self.scr["wos"], i["wo"], self.d_wos)
        self.dma("g", self.scr["W1s"][0], i["W1"][0], self.d_cv[0])
        self.dma("g", self.scr["W2s"][0], i["W2"][0], self.d_cv[0])
        self.conv_todo = list(range(1, E))

        self.wait("t", self.d_msc, d_consts)
        self.wait("t", self.c_g, g_const)
        for t_ in range(NDT):
            self.mm(PS[0:1, 0:E], self.ONECF[:], self.wr_f[:, E * t_: E * t_ + E],
                    t_ == 0, t_ == NDT - 1,
                    sem=(self.c_mm if t_ == NDT - 1 else None))
        self.wait("v", self.c_mm)
        self.op("v", lambda e: e.tensor_scalar(out=self.negcs_f[:], in0=PS[0:1, 0:E],
                                               scalar1=-1.0 / D, scalar2=None,
                                               op0=ALU.mult), self.c_v)
        self.wait("v", self.c_v)
        self.op("v", lambda e: e.tensor_copy(self.negcs_b[:], self.negcs_f[:]), self.c_v)
        self.op("v", lambda e: e.tensor_copy(self.wr_b[:], self.wr_f[:]), self.c_v)
        self.wait("v", self.c_v)
        self.op("v", lambda e: e.tensor_sub(self.wr_lo[:], self.wr_f[:], self.wr_b[:]),
                self.c_v)
        self.wait("t", self.c_v)
        self.mm(PS[:, 0:E], self.ONESF[0:1, 0:128], self.negcs_f[:], True, True,
                sem=self.c_mm)
        self.wait("v", self.c_mm)
        self.op("v", lambda e: e.tensor_copy(self.negcsb_t[:], PS[0:1, 0:E] if False
                                             else PS[:, 0:E]), self.c_v)
        self.w_negcs_v = self.c_v.n

        for e_ in range(E):
            for qtr in range(4):
                self.dmaS("y", self.rowstg[0:1, :],
                          self.ap(i["b1"], e_ * FF + 1024 * qtr, [[FF, 1], [1, 1024]]),
                          self.d_msc)
                self.wait("t", self.d_msc)
                for f8 in range(8):
                    ft = 8 * qtr + f8
                    self.mm(PS[:, 512 + f8: 512 + f8 + 1],
                            self.rowstg[0:1, 128 * f8: 128 * f8 + 128],
                            self.ONESF[0:1, 0:1], f8 == 0, f8 == 7,
                            sem=(self.c_mm if f8 == 7 else None))
                self.wait("v", self.c_mm)
                self.op("v", lambda e, ee=e_, q=qtr: e.tensor_copy(
                    self.b1c[:, 32 * ee + 8 * q: 32 * ee + 8 * q + 8],
                    PS[:, 512: 512 + 8]), self.c_v)
                self.wait("t", self.c_v)
                self.wait("y", self.c_v)

        def permrow(src_name, dst1, dst2, scale):
            self.dmaS("y", self.rowstg[0:1, 0:D], i[src_name], self.d_msc)
            self.wait("s", self.d_msc)
            ev = self.ap(self.rowstg, 0, [[D, 1], [64, 16], [2, 32]])
            od = self.ap(self.rowstg, 1, [[D, 1], [64, 16], [2, 32]])
            for (a, dst, off, sc) in ((ev, dst1, 0, scale), (od, dst1, 32, scale),
                                      (od, dst2, 0, -scale), (ev, dst2, 32, scale)):
                o = self.ap(dst, off, [[D, 1], [64, 16], [1, 32]])
                self.op("s", lambda e, a=a, o=o, k=sc:
                        e.activation(o, a, AF.Copy, scale=k), self.c_s)
            self.wait("y", self.c_s)

        permrow("bq", self.bq1r, self.bq2r, QSC)
        permrow("bk", self.bk1r, self.bk2r, QSC)
        for src, dst in (("bo", self.bo_r), ("bv", self.bv_r)):
            self.dmaS("y", self.rowstg[0:1, 0:D], i[src], self.d_msc)
            self.wait("s", self.d_msc)
            self.op("s", lambda e, d=dst: e.copy(d[0:1, :], self.rowstg[0:1, 0:D]),
                    self.c_s)
            self.wait("y", self.c_s)

        for t_ in range(NDT):
            for (wname, s1, s2, sc) in (("wq", "wq1s", "wq2s", QSC),
                                        ("wk", "wk1s", "wk2s", QSC)):
                for h2 in range(2):
                    self.dmaS("y", self.stage[:],
                              i[wname][128 * t_: 128 * t_ + 128,
                                       512 * h2: 512 * h2 + 512], self.d_msc)
                    self.wait("s", self.d_msc)
                    ev = self.ap(self.stage, 0, [[512, 128], [64, 8], [2, 32]])
                    od = self.ap(self.stage, 1, [[512, 128], [64, 8], [2, 32]])
                    lo = self.ap(self.stage_b, 0, [[512, 128], [64, 8], [1, 32]])
                    hi = self.ap(self.stage_b, 32, [[512, 128], [64, 8], [1, 32]])
                    self.op("s", lambda e, a=ev, o=lo, k=sc:
                            e.activation(o, a, AF.Copy, scale=k))
                    self.op("s", lambda e, a=od, o=hi, k=sc:
                            e.activation(o, a, AF.Copy, scale=k), self.c_s)
                    self.wait("y", self.c_s)
                    self.dmaS("y", self.scr[s1][128 * t_: 128 * t_ + 128,
                                                512 * h2: 512 * h2 + 512],
                              self.stage_b[:], self.d_msc)
                    self.wait("s", self.d_msc)
                    self.op("s", lambda e, a=od, o=lo, k=sc:
                            e.activation(o, a, AF.Copy, scale=-k))
                    self.op("s", lambda e, a=ev, o=hi, k=sc:
                            e.activation(o, a, AF.Copy, scale=k), self.c_s)
                    self.wait("y", self.c_s)
                    self.dmaS("y", self.scr[s2][128 * t_: 128 * t_ + 128,
                                                512 * h2: 512 * h2 + 512],
                              self.stage_b[:], self.d_msc)
                    self.wait("s", self.d_msc)
        self.d_wscr = self.d_msc.n

    def phase_F(self):
        PS = self.PS
        if self.nfb == 0:
            for e_c in self.conv_todo:
                self.dma("g", self.scr["W1s"][e_c], self.i["W1"][e_c], self.d_cv[e_c])
                self.dma("g", self.scr["W2s"][e_c], self.i["W2"][e_c], self.d_cv[e_c])
            self.conv_todo = []
            self.f_mm_end = self.c_mm.n
            self.f_v_end = self.c_v.n
            return
        i = self.i
        bufs = [self.xtb, self.att]
        nchunks = self.nfb * NCH
        issued = []

        def issue(k):
            fb, c_ = divmod(k, NCH)
            self.wait("g", self.c_mm, self.xtb_guard[k % 2])
            self.wait("g", self.c_s, self.xtb_guard_s[k % 2])
            self.wait("g", self.c_v, self.xtb_guard_v[k % 2])
            self.dma("g", bufs[k % 2][:],
                     self.ap(i["xT_for"], fb * D * S + 512 * c_,
                             [[S, 128], [128 * S, NDT], [1, 512]]), self.d_xt[k % 2])
            issued.append(self.d_xt[k % 2].n)

        issue(0)
        issue(1)
        for k in range(nchunks):
            fb, c_ = divmod(k, NCH)
            buf = bufs[k % 2]
            self.wait("t", self.d_xt[k % 2], issued[k])
            for t_ in range(NDT):
                self.mm(PS[0:1, 3584:4096], self.ONECB[:],
                        buf[:, 512 * t_: 512 * t_ + 512], t_ == 0, t_ == NDT - 1,
                        sem=(self.c_mm if t_ == NDT - 1 else None))
            self.wait("s", self.c_mm)
            self.op("s", lambda e: e.copy(self.mrow_b[:], PS[0:1, 3584:4096]),
                    self.c_s)
            self.wait("t", self.c_s)
            for j4 in range(4):
                for t_ in range(NDT):
                    self.mm(PS[:, 3584:3592],
                            buf[:, 512 * t_ + 128 * j4: 512 * t_ + 128 * j4 + 128],
                            self.wr_b[:, E * t_: E * t_ + E], t_ == 0, False)
                self.mm(PS[:, 3584:3592], self.mrow_b[0:1, 128 * j4: 128 * j4 + 128],
                        self.negcs_b[:], False, True, sem=self.c_mm)
                self.wait("v", self.c_mm)
                self.op("v", lambda e: e.tensor_copy(self.s_sb[:], PS[:, 3584:3592]),
                        self.c_v)
                self.wait("t", self.c_v)
                self.wait("v", self.c_v)
                self.op("v", lambda e: e.max(out=self.top_sb[:], in_=self.s_sb[:]),
                        self.c_v)
                self.wait("v", self.c_v)
                self.op("v", lambda e: e.tensor_scalar(
                    out=self.m1_sb[:], in0=self.s_sb[:], scalar1=self.top_sb[:, 0:1],
                    scalar2=None, op0=ALU.is_equal), self.c_v)
                self.op("v", lambda e: e.tensor_scalar(
                    out=self.c1_sb[:], in0=self.s_sb[:], scalar1=self.top_sb[:, 1:2],
                    scalar2=None, op0=ALU.is_equal), self.c_v)
                self.wait("v", self.c_v)
                self.op("v", lambda e: e.tensor_add(
                    self.ohacc[:, 0:8], self.ohacc[:, 0:8], self.m1_sb[:]), self.c_v)
                self.wait("v", self.c_v)
                self.op("v", lambda e: e.tensor_add(
                    self.ohacc[:, 8:16], self.ohacc[:, 8:16], self.c1_sb[:]), self.c_v)
                self.wait("v", self.c_v)
            self.xtb_guard[k % 2] = self.c_mm.n
            if k % 14 == 13 and self.conv_todo:
                e_c = self.conv_todo.pop(0)
                self.dma("g", self.scr["W1s"][e_c], i["W1"][e_c], self.d_cv[e_c])
                self.dma("g", self.scr["W2s"][e_c], i["W2"][e_c], self.d_cv[e_c])
            if k + 2 < nchunks:
                issue(k + 2)
            if c_ == NCH - 1:
                self.wait("t", self.c_v)
                self.mm(PS[0:1, 3600:3616], self.ONECF[:], self.ohacc[:], True, True,
                        sem=self.c_mm)
                self.wait("v", self.c_mm)
                self.op("v", lambda e, fb=fb: e.scalar_tensor_tensor(
                    out=self.baseoff[:], in0=PS[0:1, 3600:3616],
                    scalar=self.m28_sb[0:1, fb:fb + 1], in1=self.baseoff[:],
                    op0=ALU.mult, op1=ALU.add), self.c_v)
                self.wait("v", self.c_v)
                self.op("v", lambda e: e.memset(self.ohacc[:], 0.0), self.c_v)
                self.wait("t", self.c_v)
        for e_c in self.conv_todo:
            self.dma("g", self.scr["W1s"][e_c], i["W1"][e_c], self.d_cv[e_c])
            self.dma("g", self.scr["W2s"][e_c], i["W2"][e_c], self.d_cv[e_c])
        self.conv_todo = []
        self.mm(PS[:, 3616:3632], self.ONESF[0:1, 0:128], self.baseoff[:], True, True,
                sem=self.c_mm)
        self.wait("v", self.c_mm)
        self.op("v", lambda e: e.tensor_copy(self.offb[:], PS[:, 3616:3632]), self.c_v)
        self.f_mm_end = self.c_mm.n
        self.f_v_end = self.c_v.n

    def phase_R(self, b):
        self.wait("t", self.c_v)
        if b == 0 and "dbg_wrb_early" in self.taps:
            self.wait("s", self.c_v)
            self.dma("s", self.taps["dbg_wrb_early"], self.wr_b[:], self.d_tap)
        PS = self.PS
        i = self.i
        SX = PS[0:1, 2048:2560]
        SQ = PS[0:1, 2560:3072]
        SRT = PS[:, 3072:3080]
        bufs = [self.xtb, self.att]
        issued = {}

        def issue(c_):
            self.wait("g", self.c_mm, self.xtb_guard[c_ % 2])
            self.wait("g", self.c_s, self.xtb_guard_s[c_ % 2])
            self.wait("g", self.c_v, self.xtb_guard_v[c_ % 2])
            self.dma("g", bufs[c_ % 2][:],
                     self.ap(i["xT_own"], b * D * S + 512 * c_,
                             [[S, 128], [128 * S, NDT], [1, 512]]), self.d_xt[c_ % 2])
            issued[c_] = self.d_xt[c_ % 2].n

        issue(0)
        issue(1)
        srt_free = 0
        for c_ in range(NCH):
            buf = bufs[c_ % 2]
            self.wait("t", self.d_xt[c_ % 2], issued[c_])
            self.wait("s", self.d_xt[c_ % 2], issued[c_])
            for t_ in range(NDT):
                self.mm(SX, self.ONECB[:], buf[:, 512 * t_: 512 * t_ + 512],
                        t_ == 0, t_ == NDT - 1,
                        sem=(self.c_mm if t_ == NDT - 1 else None))
            for t_ in range(NDT):
                self.wait("s", self.c_mm)
                self.op("s", lambda e, t=t_, bu=buf: e.activation(
                    self.xsq[:], bu[:, 512 * t: 512 * t + 512], AF.Square), self.c_s)
                self.wait("t", self.c_s)
                self.mm(SQ, self.ONECB[:], self.xsq[:], t_ == 0, t_ == NDT - 1,
                        sem=self.c_mm)
            sq_mm = self.c_mm.n
            self.wait("v", self.c_mm, sq_mm)
            self.op("v", lambda e: e.tensor_copy(self.sxrow[:], SX), self.c_v)
            self.op("v", lambda e: e.tensor_scalar(
                out=self.mrow_t[:], in0=SX, scalar1=1.0 / D, scalar2=None,
                op0=ALU.mult), self.c_v)
            self.op("v", lambda e: e.tensor_scalar(
                out=self.var_t[:], in0=SQ, scalar1=1.0 / D, scalar2=None,
                op0=ALU.mult), self.c_v)
            self.wait("v", self.c_v)
            self.wait("t", self.c_v)
            self.op("v", lambda e: e.tensor_mul(self.rstd_f[:], self.mrow_t[:],
                                                self.mrow_t[:]), self.c_v)
            self.wait("v", self.c_v)
            self.op("v", lambda e: e.tensor_sub(self.var_t[:], self.var_t[:],
                                                self.rstd_f[:]), self.c_v)
            self.wait("s", self.c_v)
            self.op("s", lambda e: e.activation(self.rstd_f[:], self.var_t[:],
                                                AF.Ln,
                                                bias=self.smalls[0:1, 6:7]),
                    self.c_s)
            self.wait("s", self.c_s)
            self.op("s", lambda e: e.activation(self.rstd_f[:], self.rstd_f[:],
                                                AF.Exp, scale=-0.5), self.c_s)
            rstd_done = self.c_s.n
            self.wait("v", self.c_s, rstd_done)
            self.op("v", lambda e, c=c_: e.tensor_copy(
                self.rstd_bf[0:1, 512 * c: 512 * c + 512], self.rstd_f[:]), self.c_v)
            self.op("v", lambda e: e.tensor_mul(self.mrow_t[:], self.mrow_t[:],
                                                self.rstd_f[:]), self.c_v)
            self.wait("v", self.c_v)
            self.op("v", lambda e, c=c_: e.tensor_copy(
                self.mrstd_bf[0:1, 512 * c: 512 * c + 512], self.mrow_t[:]), self.c_v)
            self.wait("t", self.c_s, rstd_done)
            for j4 in range(4):
                self.mm(PS[:, 2560 + j4: 2561 + j4],
                        self.rstd_f[0:1, 128 * j4: 128 * j4 + 128],
                        self.ONESF[0:1, 0:1], j4 == 0, j4 == 3,
                        sem=(self.c_mm if j4 == 3 else None))
            self.wait("v", self.c_mm)
            self.op("v", lambda e, c=c_: e.tensor_copy(
                self.rstd_cols[:, 4 * c: 4 * c + 4], PS[:, 2560:2564]), self.c_v)
            rcols_v = self.c_v.n

            for j4 in range(4):
                j = 4 * c_ + j4
                self.wait("y", self.c_mm, srt_free)
                self.wait("y", self.c_v, getattr(self, "_xlo_v", 0))
                self.dma("y", self.xt32[:],
                         self.ap(i["xT_own"], b * D * S + 512 * c_ + 128 * j4,
                                 [[S, 128], [128 * S, NDT], [1, 128]]), self.d_x32)
                self.wait("v", self.d_x32)
                xhi = self.ap(buf, 128 * j4, [[NDT * 512, 128], [512, NDT], [1, 128]])
                self.op("v", lambda e, a=xhi: e.tensor_sub(
                    self.xlo[:], self.xt32[:], a), self.c_v)
                self._xlo_v = self.c_v.n
                self.wait("t", self.c_v, self._xlo_v)
                for t_ in range(NDT):
                    self.mm(SRT,
                            self.ap(buf, 512 * t_ + 128 * j4,
                                    [[NDT * 512, 128], [1, 128]]),
                            self.wr_b[:, E * t_: E * t_ + E], t_ == 0, False)
                for t_ in range(NDT):
                    self.mm(SRT,
                            self.ap(buf, 512 * t_ + 128 * j4,
                                    [[NDT * 512, 128], [1, 128]]),
                            self.wr_lo[:, E * t_: E * t_ + E], False, False)
                for t_ in range(NDT - 1):
                    self.mm(SRT, self.xlo[:, 128 * t_: 128 * t_ + 128],
                            self.wr_b[:, E * t_: E * t_ + E], False, False)
                self.mm(SRT, self.xlo[:, 128 * (NDT - 1): 128 * NDT],
                        self.wr_b[:, E * (NDT - 1): E * NDT], False, False)
                self.mm(PS[:, 3080:3081],
                        self.sxrow[0:1, 128 * j4: 128 * j4 + 128],
                        self.ONESF[0:1, 0:1], True, True, sem=self.c_mm)
                srt_free = self.c_mm.n
                self.wait("v", self.c_mm, srt_free)
                self.op("v", lambda e: e.tensor_copy(self.colB[:], PS[:, 3080:3081]),
                        self.c_v)
                self.wait("v", self.c_v)
                self.op("v", lambda e: e.scalar_tensor_tensor(
                    out=self.s_sb[:], in0=self.negcsb_t[:], scalar=self.colB[:],
                    in1=SRT, op0=ALU.mult, op1=ALU.add), self.c_v)
                self.wait("t", self.c_v)
                self.wait("v", self.c_v)
                if b == 0 and c_ == 0 and j4 == 0 and "dbg_s" in self.taps:
                    self.op("v", lambda e: e.tensor_copy(self.keep_t[:, 0:8], SRT),
                            self.c_v)
                    self.wait("s", self.c_v)
                    self.dma("s", self.taps["dbg_s"], self.s_sb[:], self.d_tap)
                    self.dma("s", self.taps["dbg_srt"], self.keep_t[:, 0:8], self.d_tap)
                    self.dma("s", self.taps["dbg_colB"], self.colB[:], self.d_tap)
                    self.dma("s", self.taps["dbg_xlo"], self.xlo[:], self.d_tap)
                    self.dma("s", self.taps["dbg_xtb"], self.xtb[:], self.d_tap)
                    self.dma("s", self.taps["dbg_negcsb"], self.negcsb_t[:], self.d_tap)
                    self.dma("s", self.taps["dbg_wrf"], self.wr_f[:], self.d_tap)
                    self.dma("s", self.taps["dbg_wrb"], self.wr_b[:], self.d_tap)
                    self.dma("s", self.taps["dbg_wrlo"], self.wr_lo[:], self.d_tap)
                self.op("v", lambda e: e.max(out=self.top_sb[:], in_=self.s_sb[:]),
                        self.c_v)
                self.wait("v", self.c_v)
                self.op("v", lambda e: e.tensor_scalar(
                    out=self.m1_sb[:], in0=self.s_sb[:], scalar1=self.top_sb[:, 0:1],
                    scalar2=None, op0=ALU.is_equal), self.c_v)
                self.wait("v", self.c_v)
                self.op("v", lambda e: e.tensor_tensor_scan(
                    out=self.c1_sb[:], data0=self.m1_sb[:], data1=self.zer8[:],
                    initial=0.0, op0=ALU.add, op1=ALU.bypass), self.c_v)
                self.wait("v", self.c_v)
                self.op("v", lambda e: e.tensor_scalar(
                    out=self.c1_sb[:], in0=self.c1_sb[:], scalar1=1.0, scalar2=None,
                    op0=ALU.is_equal), self.c_v)
                self.wait("v", self.c_v)
                self.op("v", lambda e, j=j: e.tensor_mul(
                    self.oh1_sb[:, E * j: E * j + E], self.m1_sb[:], self.c1_sb[:]),
                    self.c_v)
                self.op("v", lambda e: e.tensor_scalar(
                    out=self.colA[:], in0=self.top_sb[:, 1:2],
                    scalar1=self.top_sb[:, 0:1], scalar2=1.0,
                    op0=ALU.is_equal, op1=ALU.add), self.c_v)
                self.op("v", lambda e: e.tensor_scalar(
                    out=self.m1_sb[:], in0=self.s_sb[:], scalar1=self.top_sb[:, 1:2],
                    scalar2=None, op0=ALU.is_equal), self.c_v)
                self.wait("v", self.c_v)
                self.op("v", lambda e: e.tensor_tensor_scan(
                    out=self.c1_sb[:], data0=self.m1_sb[:], data1=self.zer8[:],
                    initial=0.0, op0=ALU.add, op1=ALU.bypass), self.c_v)
                self.wait("v", self.c_v)
                self.op("v", lambda e: e.tensor_scalar(
                    out=self.c1_sb[:], in0=self.c1_sb[:], scalar1=self.colA[:],
                    scalar2=None, op0=ALU.is_equal), self.c_v)
                self.wait("v", self.c_v)
                self.op("v", lambda e, j=j: e.tensor_mul(
                    self.oh2_sb[:, E * j: E * j + E], self.m1_sb[:], self.c1_sb[:]),
                    self.c_v)
                oh_done = self.c_v.n
                self.wait("v", self.c_v, rcols_v)
                self.op("v", lambda e: e.tensor_sub(self.colA[:], self.top_sb[:, 0:1],
                                                    self.top_sb[:, 1:2]), self.c_v)
                self.wait("v", self.c_v)
                self.op("v", lambda e, j=j: e.tensor_mul(
                    self.colA[:], self.colA[:], self.rstd_cols[:, j: j + 1]), self.c_v)
                self.wait("s", self.c_v)
                self.op("s", lambda e: e.activation(self.colB[:], self.colA[:],
                                                    AF.Exp, scale=-1.0), self.c_s)
                self.wait("v", self.c_s)
                self.op("v", lambda e: e.tensor_scalar(
                    out=self.colA[:], in0=self.colB[:], scalar1=1.0, scalar2=None,
                    op0=ALU.add), self.c_v)
                self.wait("v", self.c_v)
                self.op("v", lambda e, j=j: e.reciprocal(
                    self.w1c[:, j: j + 1], self.colA[:]), self.c_v)
                self.wait("v", self.c_v)
                self.op("v", lambda e, j=j: e.tensor_mul(
                    self.w2c[:, j: j + 1], self.colB[:], self.w1c[:, j: j + 1]),
                    self.c_v)
                self.wait("t", self.c_v, oh_done)
                self.mm(PS[:, 3104:3112], self.ltriB[:],
                        self.oh1_sb[:, E * j: E * j + E], True, False)
                self.mm(PS[:, 3112:3120], self.ltriB[:],
                        self.oh2_sb[:, E * j: E * j + E], False, False)
                self.mm(PS[:, 3104:3120], self.ONESF[0:1, 0:128], self.runoff[:],
                        False, True, sem=self.c_mm)
                self.wait("v", self.c_mm)
                self.op("v", lambda e, j=j: e.tensor_copy(
                    self.rank_sb[:, 16 * j: 16 * j + 16], PS[:, 3104:3120]), self.c_v)
                self.wait("t", self.c_v)
                self.mm(PS[0:1, 3136:3144], self.ONECB[:],
                        self.oh1_sb[:, E * j: E * j + E], True, False)
                self.mm(PS[0:1, 3144:3152], self.ONECB[:],
                        self.oh2_sb[:, E * j: E * j + E], False, True, sem=self.c_mm)
                self.wait("v", self.c_mm)
                self.op("v", lambda e: e.tensor_add(self.runoff[:], self.runoff[:],
                                                    PS[0:1, 3136:3152]), self.c_v)
                self.wait("t", self.c_v)
            self.xtb_guard[c_ % 2] = self.c_mm.n
            self.xtb_guard_s[c_ % 2] = self.c_s.n
            self.xtb_guard_v[c_ % 2] = self.c_v.n
            if c_ + 2 < NCH:
                issue(c_ + 2)

        for c_ in range(NCH):
            buf = bufs[c_ % 2]
            self.wait("g", self.c_mm, self.xtb_guard[c_ % 2])
            self.wait("g", self.c_s, self.xtb_guard_s[c_ % 2])
            self.wait("g", self.c_v)
            self.dma("g", buf[:],
                     self.ap(i["xT_own"], b * D * S + 512 * c_,
                             [[S, 128], [128 * S, NDT], [1, 512]]), self.d_xt[c_ % 2])
            self.wait("t", self.d_xt[c_ % 2])
            self.wait("t", self.c_v)
            if b == 0 and c_ == 0:
                self.wait("t", self.c_v, self.w_negcs_v)
            self.mm(PS[:, 0:512], self.ONESB[0:1, 0:128],
                    self.rstd_bf[0:1, 512 * c_: 512 * c_ + 512], True, True)
            self.mm(PS[:, 512:1024], self.ONESB[0:1, 0:128],
                    self.mrstd_bf[0:1, 512 * c_: 512 * c_ + 512], True, True,
                    sem=self.c_mm)
            self.wait("v", self.c_mm)
            self.wait("v", self.d_xt[c_ % 2])
            if self.m_h_mm_end:
                self.wait("v", self.c_mm, self.m_h_mm_end)
            for t_ in range(NDT):
                self.op("v", lambda e, t=t_, bu=buf: e.tensor_mul(
                    self.tmpa[:], bu[:, 512 * t: 512 * t + 512], PS[:, 0:512]),
                    self.c_v)
                self.wait("v", self.c_v)
                self.op("v", lambda e, t=t_, c=c_: e.tensor_sub(
                    self.xnT[:, S * t + 512 * c: S * t + 512 * c + 512],
                    self.tmpa[:], PS[:, 512:1024]), self.c_v)
                self.wait("v", self.c_v)
            self.xtb_guard[c_ % 2] = self.c_mm.n
            self.xtb_guard_v[c_ % 2] = self.c_v.n
        self.r_xnT_v = self.c_v.n
        self.r_mm_end = self.c_mm.n

    def phase_A(self, b):
        PS = self.PS
        i = self.i
        qT = self.qTp
        kT = self.kTp
        vS = self.v_sb
        qk_free = 0
        v_free = 0
        wvp_free = 0
        att_sq_free = self.r_mm_end
        att_sq_free_s = 0
        slot_free = {}
        self.wait("t", self.c_v, self.r_xnT_v)
        if self.m_y_mm_end:
            self.wait("v", self.c_mm, self.m_y_mm_end)
            self.wait("s", self.c_mm, self.m_y_mm_end)
        self.wait("s", self.c_mm, self.r_mm_end)
        self.wait("v", self.c_mm, self.r_mm_end)
        if b == 0:
            self.wait("y", self.d_msc, self.d_wscr)
            self.wait("v", self.d_cos, self.d_cos_n)

        for p in range(NPAIR):
            self.wait("g", self.c_mm, wvp_free)
            self.dma("g", self.wvp[:],
                     self.ap(i["wv"], 128 * p, [[D, 128], [128 * D, NDT], [1, 128]]),
                     self.d_wv)
            wv_done = self.d_wv.n
            self.wait("t", self.c_mm, qk_free)
            self.wait("v", self.c_mm, qk_free)
            wqp_guard = qk_free
            for (dst, snames, br1, br2) in (
                    (qT, ("wq1s", "wq2s"), self.bq1r, self.bq2r),
                    (kT, ("wk1s", "wk2s"), self.bk1r, self.bk2r)):
                self.wait("y", self.c_mm, wqp_guard)
                for w_i, sname in enumerate(snames):
                    self.dma("y", self.wqp[:, D * w_i: D * w_i + D],
                             self.ap(self.scr[sname], 128 * p,
                                     [[D, 128], [128 * D, NDT], [1, 128]]), self.d_wa)
                self.wait("t", self.d_wa)
                for c4 in range(4):
                    for t_ in range(NDT):
                        self.mm(PS[:, 3072:3584],
                                self.wqp[:, 128 * t_: 128 * t_ + 128],
                                self.xnT[:, S * t_ + 512 * c4: S * t_ + 512 * c4 + 512],
                                t_ == 0, False)
                    self.mm(PS[:, 3072:3584], br1[0:1, 128 * p: 128 * p + 128],
                            self.ONESB[0:1, 0:512], False, True)
                    for t_ in range(NDT):
                        self.mm(PS[:, 3584:4096],
                                self.wqp[:, D + 128 * t_: D + 128 * t_ + 128],
                                self.xnT[:, S * t_ + 512 * c4: S * t_ + 512 * c4 + 512],
                                t_ == 0, False)
                    self.mm(PS[:, 3584:4096], br2[0:1, 128 * p: 128 * p + 128],
                            self.ONESB[0:1, 0:512], False, True, sem=self.c_mm)
                    self.wait("v", self.c_mm)
                    self.op("v", lambda e, c=c4: e.tensor_mul(
                        self.tmpa[:], PS[:, 3072:3584],
                        self.cosr[:, 512 * c: 512 * c + 512]), self.c_v)
                    self.op("v", lambda e, c=c4: e.tensor_mul(
                        self.xsq[:], PS[:, 3584:4096],
                        self.sinr[:, 512 * c: 512 * c + 512]), self.c_v)
                    self.wait("v", self.c_v)
                    self.op("v", lambda e, c=c4, d=dst: e.tensor_add(
                        d[:, 512 * c: 512 * c + 512], self.tmpa[:], self.xsq[:]),
                        self.c_v)
                    self.wait("t", self.c_v)
                wqp_guard = self.c_mm.n
            rope_v = self.c_v.n
            self.wait("g", self.c_mm, v_free)
            zap = self.ap(vS, 64, [[130 * NTOKM, 128], [130, NTOKM], [65, 2], [1, 1]])
            self.op("g", lambda e, a=zap: e.memset(a, 1.0), self.c_g)
            vz = self.c_g.n
            self.wait("v", self.c_mm, v_free)
            self.wait("v", self.c_g, vz)
            self.wait("t", self.d_wv, wv_done)
            for j in range(NTOKM):
                for t_ in range(NDT):
                    self.mm(PS[:, 3072:3200],
                            self.xnT[:, S * t_ + 128 * j: S * t_ + 128 * j + 128],
                            self.wvp[:, 128 * t_: 128 * t_ + 128], t_ == 0, False)
                self.mm(PS[:, 3072:3200], self.ONESB[0:1, 0:128],
                        self.bv_r[0:1, 128 * p: 128 * p + 128], False, True,
                        sem=self.c_mm)
                self.wait("v", self.c_mm)
                oap = self.ap(vS, 130 * j, [[130 * NTOKM, 128], [65, 2], [1, 64]])
                self.op("v", lambda e, o=oap: e.tensor_copy(o, PS[:, 3072:3200]),
                        self.c_v)
                self.wait("t", self.c_v)
            self.wait("v", self.c_mm, att_sq_free)
            if att_sq_free_s:
                self.wait("v", self.c_s, att_sq_free_s)
            for (src, col) in ((qT, 0), (kT, 1)):
                self.op("v", lambda e, sr=src: e.tensor_mul(
                    self.att[:, 0:2048], sr[:], sr[:]), self.c_v)
                self.wait("t", self.c_v)
                for c4 in range(4):
                    self.mm(PS[0:2, 3584:4096], self.selq[:],
                            self.att[:, 512 * c4: 512 * c4 + 512], True, True,
                            sem=self.c_mm)
                    self.wait("v", self.c_mm)
                    self.op("v", lambda e, c=c4: e.reduce_max(
                        self.smalls[0:2, 8 + c: 9 + c], PS[0:2, 3584:4096],
                        axis=AX.X), self.c_v)
                    self.wait("t", self.c_v)
                self.wait("v", self.c_v)
                self.op("v", lambda e, cl=col: e.reduce_max(
                    self.smalls[0:2, cl: cl + 1], self.smalls[0:2, 8:12], axis=AX.X),
                    self.c_v)
                self.wait("v", self.c_v)
            self.op("v", lambda e: e.tensor_add(self.smalls[0:2, 2:3],
                                                self.smalls[0:2, 0:1],
                                                self.smalls[0:2, 1:2]), self.c_v)
            self.wait("t", self.c_v)
            for h in range(2):
                self.mm(PS[0:1, 3584 + h: 3585 + h],
                        self.smalls[h:h + 1, 2:3] if h == 0 else None,
                        self.ONESF[0:1, 0:1], h == 0, h == 1, sem=None) \
                    if False else None
            self.mm(PS[0:1, 3584:3586], self.smalls[0:2, 2:3], self.ltriF[0:2, 0:2],
                    True, True, sem=self.c_mm)
            self.wait("s", self.c_mm)
            self.op("s", lambda e: e.activation(self.negu_r[0:1, 0:1],
                                                PS[0:1, 3584:3585], AF.Copy,
                                                scale=-0.5), self.c_s)
            self.wait("s", self.c_s)
            self.op("s", lambda e: e.activation(self.colA[0:1, 0:1],
                                                PS[0:1, 3585:3586], AF.Copy,
                                                scale=-0.5), self.c_s)
            self.wait("v", self.c_s)
            self.op("v", lambda e: e.tensor_sub(self.negu_r[0:1, 1:2],
                                                self.colA[0:1, 0:1],
                                                self.negu_r[0:1, 0:1]), self.c_v)
            self.wait("t", self.c_v)
            for h in range(2):
                self.mm(PS[:, 3586 + h: 3587 + h], self.ONESB[0:1, 0:128],
                        self.negu_r[0:1, h: h + 1], True, True,
                        sem=(self.c_mm if h == 1 else None))
            self.wait("v", self.c_mm)
            self.op("v", lambda e: e.tensor_copy(self.smalls[:, 4:6],
                                                 PS[:, 3586:3588]), self.c_v)
            negu_v = self.c_v.n

            exp_last = {0: 0, 1: 0}
            bank0_v_free = 0
            for Qb in range(4):
                njt = 4 * Qb + 4
                for J in range(njt):
                    par = J % 2
                    for h in range(2):
                        bk = par if h == 0 else 2 + par
                        if exp_last[h]:
                            self.wait("t", self.c_s, exp_last[h])
                        if h == 0 and par == 0 and bank0_v_free:
                            self.wait("t", self.c_v, bank0_v_free)
                        self.mm(PS[:, 512 * bk: 512 * bk + 512],
                                kT[64 * h: 64 * h + 64, 128 * J: 128 * J + 128],
                                qT[64 * h: 64 * h + 64, 512 * Qb: 512 * Qb + 512],
                                True, True, tp=(64 * h, 0),
                                sem=(self.c_mm if h == 1 else None))
                    sc_mm = self.c_mm.n
                    if par == 1:
                        J0 = J - 1
                        sl = (J0 // 2) % 2
                        for h in range(2):
                            bk0 = 0 if h == 0 else 2
                            self.wait("s", self.c_mm, sc_mm)
                            self.wait("s", self.c_v, negu_v)
                            gd = slot_free.get((h, sl), 0)
                            if gd:
                                self.wait("s", self.c_mm, gd)
                            self.op("s", lambda e, h=h, b0=bk0, sl=sl: e.activation(
                                self.att[:, 2048 * h + 1024 * sl:
                                         2048 * h + 1024 * sl + 1024],
                                PS[:, 512 * b0: 512 * b0 + 1024], AF.Exp,
                                bias=self.smalls[:, 4 + h: 5 + h]), self.c_s)
                            exp_last[h] = self.c_s.n
                        exp_done = self.c_s.n
                        mask_v = 0
                        for JJ in (J0, J):
                            m = JJ - 4 * Qb
                            if m >= 0:
                                off = 1024 * sl + 512 * (JJ - J0)
                                for h in range(2):
                                    self.wait("v", self.c_s, exp_done)
                                    self.op("v", lambda e, h=h, o=off, m=m:
                                            e.tensor_mul(
                                        self.att[:, 2048 * h + o + 128 * m:
                                                 2048 * h + o + 128 * m + 128],
                                        self.att[:, 2048 * h + o + 128 * m:
                                                 2048 * h + o + 128 * m + 128],
                                        self.ltriB[:]), self.c_v)
                                mask_v = self.c_v.n
                        self.wait("t", self.c_s, exp_done)
                        if mask_v:
                            self.wait("t", self.c_v, mask_v)
                        for JJ in (J0, J):
                            m = JJ - 4 * Qb
                            lo = max(0, 128 * m)
                            off = 1024 * sl + 512 * (JJ - J0)
                            for h in range(2):
                                self.mm(PS[0:65, 512 * (4 + h) + lo:
                                           512 * (4 + h) + 512],
                                        self.ap(vS, 130 * JJ + 65 * h,
                                                [[130 * NTOKM, 128], [1, 65]]),
                                        self.att[:, 2048 * h + off + lo:
                                                 2048 * h + off + 512],
                                        JJ == 0, JJ == njt - 1,
                                        sem=(self.c_mm if (JJ == J and h == 1)
                                             else None))
                        slot_free[(0, sl)] = self.c_mm.n
                        slot_free[(1, sl)] = self.c_mm.n
                av_mm = self.c_mm.n
                for h in range(2):
                    self.wait("v", self.c_mm, av_mm)
                    if h == 1:
                        self.wait("v", self.c_v)
                    self.op("v", lambda e, h=h: e.reciprocal(
                        self.zrec[:], PS[64:65, 512 * (4 + h): 512 * (4 + h) + 512]),
                        self.c_v)
                    self.wait("v", self.c_v)
                    self.op("v", lambda e: e.tensor_copy(self.zrec_b[:], self.zrec[:]),
                            self.c_v)
                    zb = self.c_v.n
                    self.wait("t", self.c_v, zb)
                    self.wait("t", self.c_s, exp_last[0])
                    if bank0_v_free:
                        self.wait("t", self.c_v, bank0_v_free)
                    self.mm(PS[0:64, 0:512], self.ONESB[0:1, 0:64], self.zrec_b[:],
                            True, True, sem=self.c_mm)
                    zmm = self.c_mm.n
                    self.wait("s", self.c_mm, zmm)
                    self.wait("s", self.c_v)
                    self.op("s", lambda e, h=h: e.copy(
                        self.aun[0:64, :],
                        PS[0:64, 512 * (4 + h): 512 * (4 + h) + 512]), self.c_s)
                    self.wait("v", self.c_s)
                    hg = 2 * p + h
                    self.op("v", lambda e, hg=hg, Qb=Qb: e.tensor_mul(
                        self.AH[64 * (hg % 2): 64 * (hg % 2) + 64,
                                S * (hg // 2) + 512 * Qb:
                                S * (hg // 2) + 512 * Qb + 512],
                        self.aun[0:64, :], PS[0:64, 0:512]), self.c_v)
                    bank0_v_free = self.c_v.n
                    self.wait("t", self.c_v)
            last_pair = self.c_mm.n
            qk_free = last_pair
            v_free = last_pair
            wvp_free = last_pair
            att_sq_free = last_pair
            att_sq_free_s = self.c_s.n
        self.a_heads_v = self.c_v.n
        self.xtb_guard[1] = self.c_mm.n
        self.xtb_guard_s[1] = self.c_s.n
        self.xtb_guard_v[1] = self.c_v.n

        wo_free = [0, 0]
        self.wait("t", self.d_wos, 16)
        op_last_v = 0
        for g4 in range(4):
            for dv in range(NDT):
                wb = self.wop
                self.wait("y", self.c_mm, wo_free[0])
                self.dma("y", wb[:],
                         self.ap(self.scr["wos"], 128 * dv * D, [[D, 128], [1, D]]),
                         self.d_wo)
                self.wait("t", self.d_wo)
                if dv == 0:
                    self.wait("t", self.c_v, self.a_heads_v)
                    if op_last_v:
                        self.wait("t", self.c_v, op_last_v)
                for jj_i in range(4):
                    jj = 4 * g4 + jj_i
                    for half in range(2):
                        self.mm(PS[:, 1024 * jj_i + 512 * half:
                                   1024 * jj_i + 512 * half + 512],
                                self.AH[:, S * dv + 128 * jj: S * dv + 128 * jj + 128],
                                wb[:, 512 * half: 512 * half + 512],
                                dv == 0, False,
                                sem=(self.c_mm if (jj_i == 3 and half == 1)
                                     else None))
                wo_free[0] = self.c_mm.n
            for jj_i in range(4):
                for half in range(2):
                    self.mm(PS[:, 1024 * jj_i + 512 * half:
                               1024 * jj_i + 512 * half + 512],
                            self.ONESB[0:1, 0:128],
                            self.bo_r[0:1, 512 * half: 512 * half + 512],
                            False, True,
                            sem=(self.c_mm if (jj_i == 3 and half == 1) else None))
            op_mm = self.c_mm.n
            self.wait("v", self.c_mm, op_mm)
            if g4 == 0 and self.o_prev_v:
                self.wait("v", self.c_v, self.o_prev_v)
            for jj_i in range(4):
                jj = 4 * g4 + jj_i
                self.op("v", lambda e, jj=jj, ji=jj_i: e.tensor_copy(
                    self.sumT[:, 1024 * jj: 1024 * jj + 1024],
                    PS[:, 1024 * ji: 1024 * ji + 1024]), self.c_v)
            op_last_v = self.c_v.n
        self.a_outproj_mm = self.c_mm.n
        self.a_outproj_v = self.c_v.n

    def phase_M(self, b):
        PS = self.PS
        i = self.i
        self.wait("v", self.c_mm, self.f_mm_end)
        self.wait("v", self.c_v, self.f_v_end)
        for j in range(NTOKM):
            self.op("v", lambda e, j=j: e.tensor_add(
                self.keep_t[:], self.rank_sb[:, 16 * j: 16 * j + 16], self.offb[:]),
                self.c_v)
            self.wait("v", self.c_v)
            self.op("v", lambda e: e.tensor_scalar(
                out=self.keep_t[:], in0=self.keep_t[:], scalar1=CAP, scalar2=None,
                op0=ALU.is_le), self.c_v)
            self.wait("v", self.c_v)
            self.op("v", lambda e, j=j: e.tensor_mul(
                self.m1_sb[:], self.oh1_sb[:, E * j: E * j + E], self.keep_t[:, 0:8]),
                self.c_v)
            self.op("v", lambda e, j=j: e.tensor_mul(
                self.c1_sb[:], self.oh2_sb[:, E * j: E * j + E],
                self.keep_t[:, 8:16]), self.c_v)
            self.wait("v", self.c_v)
            self.op("v", lambda e, j=j: e.tensor_scalar(
                out=self.m1_sb[:], in0=self.m1_sb[:], scalar1=self.w1c[:, j: j + 1],
                scalar2=None, op0=ALU.mult), self.c_v)
            self.op("v", lambda e, j=j: e.tensor_scalar(
                out=self.c1_sb[:], in0=self.c1_sb[:], scalar1=self.w2c[:, j: j + 1],
                scalar2=None, op0=ALU.mult), self.c_v)
            self.wait("v", self.c_v)
            self.op("v", lambda e, j=j: e.tensor_add(
                self.w8_sb[:, E * j: E * j + E], self.m1_sb[:], self.c1_sb[:]),
                self.c_v)
            self.wait("v", self.c_v)
        w8_v = self.c_v.n

        w1bufs = [self.w1blk, self.w1blk2]
        w2bufs = [self.w2blk, self.w2blk2]
        w1_free = [0, 0]
        w2_free = [0, 0]
        b2_mm_free = 0
        y_ev = 0
        h_bank_s = {}
        for e_ in range(E):
            self.wait("g", self.c_mm, b2_mm_free)
            self.dma("g", self.b2_r[0:1, :], i["b2"][e_], self.d_b2)
            b2_dma = self.d_b2.n
            self.wait("t", self.d_cv[e_], 32)
            for c_ in range(NCH):
                for fb16 in range(NFT // 2):
                    wb = w1bufs[fb16 % 2]
                    self.wait("s", self.c_mm, w1_free[fb16 % 2])
                    self.dma("s", wb[:],
                             self.ap(self.scr["W1s"], e_ * D * FF + 256 * fb16,
                                     [[FF, 128], [128 * FF, NDT], [1, 256]]),
                             self.d_w1[fb16 % 2])
                    self.wait("t", self.d_w1[fb16 % 2])
                    if e_ == 0 and c_ == 0 and fb16 == 0:
                        self.wait("t", self.c_mm, self.a_outproj_mm)
                        self.wait("s", self.c_mm, self.a_outproj_mm)
                        self.wait("t", self.c_v, self.a_outproj_v)
                    for f2 in range(2):
                        ft = 2 * fb16 + f2
                        bk = ft % 4
                        gd = h_bank_s.get(bk, 0)
                        if gd:
                            self.wait("t", self.c_s, gd)
                        if ft == 0 and y_ev:
                            self.wait("t", self.c_v, y_ev)
                        for t_ in range(NDT):
                            self.mm(PS[:, 512 * bk: 512 * bk + 512],
                                    wb[:, 256 * t_ + 128 * f2:
                                       256 * t_ + 128 * f2 + 128],
                                    self.xnT[:, S * t_ + 512 * c_:
                                             S * t_ + 512 * c_ + 512],
                                    t_ == 0, t_ == NDT - 1,
                                    sem=(self.c_mm if t_ == NDT - 1 else None))
                        hm = self.c_mm.n
                        self.wait("s", self.c_mm, hm)
                        self.op("s", lambda e, ft=ft, ee=e_, bk=bk: e.activation(
                            self.AH[:, 512 * ft: 512 * ft + 512],
                            PS[:, 512 * bk: 512 * bk + 512], AF.Gelu,
                            bias=self.b1c[:, 32 * ee + ft: 32 * ee + ft + 1]),
                            self.c_s)
                        h_bank_s[bk] = self.c_s.n
                    w1_free[fb16 % 2] = self.c_mm.n
                h_s = self.c_s.n
                self.m_h_mm_end = self.c_mm.n
                self.wait("t", self.c_s, h_s)
                for ft in range(NFT):
                    wb2 = w2bufs[ft % 2]
                    self.wait("y", self.c_mm, w2_free[ft % 2])
                    self.dma("y", wb2[:],
                             self.ap(self.scr["W2s"], e_ * FF * D + 128 * ft * D,
                                     [[D, 128], [1, D]]), self.d_w2[ft % 2])
                    self.wait("t", self.d_w2[ft % 2])
                    for tm in range(4):
                        for half in range(2):
                            bk = 2 * tm + half
                            self.mm(PS[:, 512 * bk: 512 * bk + 512],
                                    self.AH[:, 512 * ft + 128 * tm:
                                             512 * ft + 128 * tm + 128],
                                    wb2[:, 512 * half: 512 * half + 512],
                                    ft == 0, False,
                                    sem=(self.c_mm if (tm == 3 and half == 1)
                                         else None))
                    w2_free[ft % 2] = self.c_mm.n
                    if ft == 0:
                        self.wait("t", self.d_b2, b2_dma)
                        for tm in range(4):
                            for half in range(2):
                                bk = 2 * tm + half
                                self.mm(PS[:, 512 * bk: 512 * bk + 512],
                                        self.ONESB[0:1, 0:128],
                                        self.b2_r[0:1, 512 * half: 512 * half + 512],
                                        False, False,
                                        sem=(self.c_mm if (tm == 3 and half == 1)
                                             else None))
                        w2_free[ft % 2] = self.c_mm.n
                y_mm = self.c_mm.n
                self.wait("v", self.c_mm, y_mm)
                self.wait("v", self.c_v, w8_v)
                self.wait("v", self.c_v)
                for tm in range(4):
                    j = 4 * c_ + tm
                    for half in range(2):
                        bk = 2 * tm + half
                        self.op("v", lambda e, j=j, h2=half, bk=bk, ee=e_:
                                e.scalar_tensor_tensor(
                                    out=self.sumT[:, 1024 * j + 512 * h2:
                                                  1024 * j + 512 * h2 + 512],
                                    in0=PS[:, 512 * bk: 512 * bk + 512],
                                    scalar=self.w8_sb[:, E * j + ee: E * j + ee + 1],
                                    in1=self.sumT[:, 1024 * j + 512 * h2:
                                                  1024 * j + 512 * h2 + 512],
                                    op0=ALU.mult, op1=ALU.add),
                                self.c_v if (tm == 3 and half == 1) else None)
                y_ev = self.c_v.n
                self.c_ev.inc()
            b2_mm_free = self.c_mm.n
        self.m_y_mm_end = self.c_mm.n
        self.m_done_v = self.c_v.n
        self.m_y_ev = y_ev

    def phase_O(self, b):
        i = self.i
        bufs = [self.xo_t, self.xo_t2]
        free_st = self.o_free_st
        self.wait("v", self.c_v, self.m_done_v)
        for j in range(NTOKM):
            sl = j % 2
            bu = bufs[sl]
            self.wait("y", self.d_st[sl], free_st[sl])
            self.dma("y", bu[:], i["x_own"][b, 128 * j: 128 * j + 128, :],
                     self.d_xo[sl])
            self.wait("v", self.d_xo[sl])
            self.op("v", lambda e, bu=bu, j=j: e.tensor_add(
                bu[:], bu[:], self.sumT[:, 1024 * j: 1024 * j + 1024]), self.c_v)
            self.wait("s", self.c_v)
            self.dma("s", self.y_out[b, 128 * j: 128 * j + 128, :], bu[:],
                     self.d_st[sl])
            free_st[sl] = self.d_st[sl].n
        self.o_prev_v = self.c_v.n

    def tap_dump(self, name, src_ap):
        self.wait("s", self.c_v)
        self.wait("s", self.c_s)
        self.dma("s", self.taps[name], src_ap, self.d_tap)

    def finish(self):
        self.wait("y", self.d_st0, self.d_st0.n)
        self.wait("y", self.d_st1, self.d_st1.n)
        self.wait("y", self.d_tap, self.d_tap.n)
        nc = self.nc
        q = self.q
        with nc.Block() as block:
            @block.tensor
            def _(e):
                for f in q["t"]:
                    f(e)

            @block.vector
            def _(e):
                for f in q["v"]:
                    f(e)

            @block.scalar
            def _(e):
                for f in q["s"]:
                    f(e)

            @block.gpsimd
            def _(e):
                for f in q["g"]:
                    f(e)

            @block.sync
            def _(e):
                for f in q["y"]:
                    f(e)
        self.es.__exit__(None, None, None)
        return nc


def build_full(nb=4, nfb=28, taps=(), upto=None):
    B = Builder(nb=nb, nfb=nfb, taps=taps)
    B.phase_W()
    B.phase_F()
    for b in range(nb):
        B.phase_R(b)
        if upto == "R":
            break
        B.phase_A(b)
        if upto == "A":
            break
        B.phase_M(b)
        if upto == "M":
            break
        B.phase_O(b)
    return B


def make_consts():
    import ml_dtypes
    inv = 10000.0 ** (-np.arange(0, HD, 2, dtype=np.float64) / HD)
    pos = np.arange(S, dtype=np.float64)
    fr = pos[:, None] * inv[None, :]
    cos32 = np.cos(fr).astype(np.float32)
    sin32 = np.sin(fr).astype(np.float32)
    r = np.arange(128) % 32
    cosrep = np.ascontiguousarray(cos32[:, r].T)
    sinrep = np.ascontiguousarray(sin32[:, r].T)
    k = np.arange(128)
    ltri = (k[:, None] <= k[None, :]).astype(np.float32)
    return {
        "cosrep": cosrep,
        "sinrep": sinrep,
        "ltri_b": ltri.astype(ml_dtypes.bfloat16),
        "ltri_f": ltri,
    }


def make_in_maps(x, weights, nb=4, nfb=28):
    xT = np.ascontiguousarray(x.transpose(0, 2, 1))
    shared = dict(weights)
    shared.update(make_consts())
    in_maps = []
    for c in range(NCORES):
        own = list(range(nb * c, nb * c + nb)) if x.shape[0] == NCORES * nb \
            else list(range(nb))
        forn = [bb for bb in range(x.shape[0]) if bb // nb != c] if nfb else []
        im = dict(shared)
        im["x_own"] = np.ascontiguousarray(x[own])
        im["xT_own"] = np.ascontiguousarray(xT[own])
        if nfb:
            im["xT_for"] = np.ascontiguousarray(xT[forn])
            im["m28"] = np.array([[1.0 if fb < nb * c else 0.0 for fb in forn]],
                                 np.float32)
        else:
            im["m28"] = np.zeros((1, 1), np.float32)
        in_maps.append(im)
    return in_maps


def kernel(L0_attn_x, L0_attn_wq, L0_attn_bq, L0_attn_wk, L0_attn_bk,
           L0_attn_wv, L0_attn_bv, L0_attn_wo, L0_attn_bo,
           L0_ffn_Wr, L0_ffn_W1, L0_ffn_b1, L0_ffn_W2, L0_ffn_b2):
    x = np.ascontiguousarray(np.asarray(L0_attn_x, dtype=np.float32))
    weights = {
        "wq": np.ascontiguousarray(L0_attn_wq, np.float32),
        "bq": np.ascontiguousarray(L0_attn_bq, np.float32).reshape(1, D),
        "wk": np.ascontiguousarray(L0_attn_wk, np.float32),
        "bk": np.ascontiguousarray(L0_attn_bk, np.float32).reshape(1, D),
        "wv": np.ascontiguousarray(L0_attn_wv, np.float32),
        "bv": np.ascontiguousarray(L0_attn_bv, np.float32).reshape(1, D),
        "wo": np.ascontiguousarray(L0_attn_wo, np.float32),
        "bo": np.ascontiguousarray(L0_attn_bo, np.float32).reshape(1, D),
        "Wr": np.ascontiguousarray(L0_ffn_Wr, np.float32),
        "W1": np.ascontiguousarray(L0_ffn_W1, np.float32),
        "b1": np.ascontiguousarray(L0_ffn_b1, np.float32),
        "W2": np.ascontiguousarray(L0_ffn_W2, np.float32),
        "b2": np.ascontiguousarray(L0_ffn_b2, np.float32),
    }
    B = build_full(nb=4, nfb=28)
    nc = B.finish()
    in_maps = make_in_maps(x, weights, nb=4, nfb=28)
    res = run_bass_kernel_spmd(nc, in_maps, core_ids=list(range(NCORES)))
    out = np.empty((32, S, D), np.float32)
    for c in range(NCORES):
        out[4 * c: 4 * c + 4] = res.results[c]["y_out"]
    return out


# version 08 (final)
# speedup vs baseline: 1.3689x; 1.3689x over previous
import math
import numpy as np
from contextlib import ExitStack

import concourse.bass as bass
import concourse.mybir as mybir
from concourse.bass_utils import run_bass_kernel_spmd

F32 = mybir.dt.float32
BF16 = mybir.dt.bfloat16
AF = mybir.ActivationFunctionType
ALU = mybir.AluOpType
AX = mybir.AxisListType

NCORES = 8
S = 2048
D = 1024
HD = 64
NPAIR = 8
FF = 4096
E = 8
CAP = 10240.0
QSC = 1.0 / math.sqrt(math.sqrt(float(HD)))
NDT = D // 128
NFT = FF // 128
NTOKM = S // 128
NCH = S // 512


class Sem:
    def __init__(self, h):
        self.h = h
        self.n = 0

    def inc(self, k=1):
        self.n += k
        return self.n


class Builder:
    def __init__(self, nb=4, nfb=28, taps=()):
        self.nb = nb
        self.nfb = nfb
        self.nc = bass.Bass("TRN2", target_bir_lowering=False)
        self.q = {k: [] for k in ("t", "v", "s", "g", "y")}
        self.taps = {}
        self.es = ExitStack()
        self.es.__enter__()
        self._declare_dram(taps)
        self._declare_sbuf()
        self._declare_sems()
        self.o_prev_v = 0
        self.m_y_mm_end = 0
        self.m_h_mm_end = 0
        self.xtb_guard = [0, 0]
        self.xtb_guard_s = [0, 0]
        self.xtb_guard_v = [0, 0]
        self.o_free_st = [0, 0]
        self.r_mm_end = 0

    def _declare_dram(self, taps):
        nc = self.nc
        i = {}

        def din(name, shape, dtype=F32):
            i[name] = nc.dram_tensor(name, shape, dtype, kind="ExternalInput").ap()

        din("x_own", [self.nb, S, D])
        din("xT_own", [self.nb, D, S])
        if self.nfb:
            din("xT_for", [self.nfb, D, S])
        din("m28", [1, max(self.nfb, 1)])
        for w in ("wq", "wk", "wv", "wo"):
            din(w, [D, D])
        for b in ("bq", "bk", "bv", "bo"):
            din(b, [1, D])
        din("Wr", [D, E])
        din("W1", [E, D, FF])
        din("b1", [E, FF])
        din("W2", [E, FF, D])
        din("b2", [E, D])
        din("cosrep", [128, S])
        din("sinrep", [128, S])
        din("ltri_b", [128, 128], BF16)
        din("ltri_f", [128, 128])
        self.i = i
        self.y_out = nc.dram_tensor("y_out", [self.nb, S, D], F32,
                                    kind="ExternalOutput").ap()
        scr = {}
        for n in ("wq1s", "wq2s", "wk1s", "wk2s", "wos"):
            scr[n] = nc.dram_tensor(n, [D, D], BF16, kind="Internal").ap()
        scr["W1s"] = nc.dram_tensor("W1s", [E, D, FF], BF16, kind="Internal").ap()
        scr["W2s"] = nc.dram_tensor("W2s", [E, FF, D], BF16, kind="Internal").ap()
        self.scr = scr
        for name, shape, dtyp in taps:
            self.taps[name] = nc.dram_tensor(name, shape, dtyp,
                                             kind="ExternalOutput").ap()

    def _declare_sbuf(self):
        nc = self.nc

        def sb(name, shape, dtype):
            t = self.es.enter_context(nc.sbuf_tensor(name, shape, dtype))
            setattr(self, name, t)

        sb("xnT", [128, NDT * S], BF16)
        sb("sumT", [128, NTOKM * D], BF16)
        sb("AH", [128, NDT * S], BF16)
        sb("cosr", [128, S], BF16)
        sb("sinr", [128, S], BF16)
        sb("ONESB", [1, 512], BF16)
        sb("ONECB", [128, 1], BF16)
        sb("ONECF", [128, 1], F32)
        sb("ltriB", [128, 128], BF16)
        sb("ltriF", [128, 128], F32)
        self.ONESF = self.ltriF
        sb("selq", [128, 2], BF16)
        sb("wr_f", [128, NDT * E], F32)
        sb("wr_b", [128, NDT * E], BF16)
        sb("wr_lo", [128, NDT * E], BF16)
        sb("negcs_f", [1, E], F32)
        sb("negcs_b", [1, E], BF16)
        sb("negcsb_t", [128, E], F32)
        sb("m28_sb", [1, max(self.nfb, 1)], F32)
        sb("bq1r", [1, D], BF16)
        sb("bq2r", [1, D], BF16)
        sb("bk1r", [1, D], BF16)
        sb("bk2r", [1, D], BF16)
        sb("bo_r", [1, D], BF16)
        sb("bv_r", [1, D], BF16)
        sb("b2_r", [1, D], BF16)
        sb("b1c", [128, E * 32], F32)
        sb("rank_sb", [128, NTOKM * 16], F32)
        sb("oh1_sb", [128, NTOKM * E], BF16)
        sb("oh2_sb", [128, NTOKM * E], BF16)
        sb("w1c", [128, NTOKM], F32)
        sb("w2c", [128, NTOKM], F32)
        sb("w8_sb", [128, NTOKM * E], F32)
        sb("runoff", [1, 16], F32)
        sb("baseoff", [1, 16], F32)
        sb("offb", [128, 16], F32)
        sb("ohacc", [128, 16], F32)
        sb("keep_t", [128, 16], F32)
        sb("rstd_cols", [128, NTOKM], F32)
        sb("smalls", [128, 16], F32)
        sb("negu_r", [1, 8], BF16)
        sb("xtb", [128, NDT * 512], BF16)
        sb("xt32", [128, NDT * 128], F32)
        sb("xlo", [128, NDT * 128], BF16)
        sb("xsq", [128, 512], BF16)
        sb("sxrow", [1, 512], F32)
        sb("mrow_t", [1, 512], F32)
        sb("var_t", [1, 512], F32)
        sb("rstd_f", [1, 512], F32)
        sb("rstd_bf", [1, S], BF16)
        sb("mrstd_bf", [1, S], BF16)
        sb("tmpa", [128, 512], F32)
        sb("s_sb", [128, E], F32)
        sb("top_sb", [128, E], F32)
        sb("m1_sb", [128, E], F32)
        sb("c1_sb", [128, E], F32)
        sb("zer8", [128, E], F32)
        sb("colA", [128, 1], F32)
        sb("colB", [128, 1], F32)
        sb("mrow_b", [1, 512], BF16)
        sb("qTp", [128, S], BF16)
        sb("kTp", [128, S], BF16)
        sb("v_sb", [128, NTOKM * 130], BF16)
        sb("wqp", [128, 2 * D], BF16)
        sb("wvp", [128, D], BF16)
        sb("att", [128, 4096], BF16)
        sb("aun", [128, 512], BF16)
        sb("wop", [128, D], BF16)
        sb("w1blk", [128, NDT * 256], BF16)
        sb("w1blk2", [128, NDT * 256], BF16)
        sb("w2blk", [128, D], BF16)
        sb("w2blk2", [128, D], BF16)
        sb("xo_t", [128, D], F32)
        sb("xo_t2", [128, D], F32)
        self.stage = self.tmpa
        self.stage_b = self.xsq
        self.rowstg = self.xt32
        self.zrec = self.rstd_f
        self.zrec_b = self.mrow_b
        self.PS = self.es.enter_context(nc.psum_tensor("PS", [128, 4096], F32))

    def _declare_sems(self):
        names = ("d_wa d_wo d_xt0 d_xt1 d_w10 d_w11 d_w20 d_w21 d_xo0 d_xo1 "
                 "d_st0 d_st1 d_msc d_x32 d_wv d_cos d_b2 d_wos d_tap "
                 "c_v c_s c_g c_mm c_ev").split()
        names += ["d_cv%d" % e for e in range(E)]
        for n in names:
            setattr(self, n, Sem(self.es.enter_context(self.nc.semaphore(n))))
        self.d_xt = [self.d_xt0, self.d_xt1]
        self.d_w1 = [self.d_w10, self.d_w11]
        self.d_w2 = [self.d_w20, self.d_w21]
        self.d_xo = [self.d_xo0, self.d_xo1]
        self.d_st = [self.d_st0, self.d_st1]
        self.d_cv = [getattr(self, "d_cv%d" % e) for e in range(E)]

    def em(self, eng, f):
        self.q[eng].append(f)

    def wait(self, eng, sem, n=None):
        v = sem.n if n is None else n
        if v > 0:
            self.em(eng, lambda e, h=sem.h, k=v: e.wait_ge(h, k))

    def dma(self, eng, out, in_, sem):
        self.em(eng, lambda e, o=out, i=in_, h=sem.h:
                e.dma_start(out=o, in_=i).then_inc(h, 16))
        return sem.inc(16)

    def dmaS(self, eng, out, in_, sem):
        n = self.dma(eng, out, in_, sem)
        self.wait(eng, sem, n)
        return n

    def mm(self, out, lhsT, rhs, start, stop, sem=None, tp=None):
        if sem is None:
            self.em("t", lambda e, o=out, l=lhsT, r=rhs, a=start, b=stop, t=tp:
                    e.matmul(o, l, r, start=a, stop=b, tile_position=t))
            return None
        self.em("t", lambda e, o=out, l=lhsT, r=rhs, a=start, b=stop, t=tp, h=sem.h:
                e.matmul(o, l, r, start=a, stop=b, tile_position=t).then_inc(h, 1))
        return sem.inc()

    def op(self, eng, f, sem=None):
        if sem is None:
            self.em(eng, lambda e, g=f: g(e))
            return None
        self.em(eng, lambda e, g=f, h=sem.h: g(e).then_inc(h, 1))
        return sem.inc()

    @staticmethod
    def ap(t, off, dims):
        a = t if isinstance(t, bass.AP) else t[:]
        return bass.AP(a.tensor, a.offset + off, dims)

    def phase_W(self):
        PS = self.PS
        i = self.i
        for f in (
            lambda e: e.memset(self.ONESB[:], 1.0),
            lambda e: e.memset(self.ONECB[:], 1.0),
            lambda e: e.memset(self.ONECF[:], 1.0),
            lambda e: e.memset(self.zer8[:], 0.0),
            lambda e: e.memset(self.runoff[:], 0.0),
            lambda e: e.memset(self.baseoff[:], 0.0),
            lambda e: e.memset(self.ohacc[:], 0.0),
            lambda e: e.memset(self.offb[:], 0.0),
            lambda e: e.memset(self.smalls[:, 6:7], 1e-5),
            lambda e: e.memset(self.selq[:], 0.0),
            lambda e: e.memset(self.selq[0:64, 0:1], 1.0),
        ):
            self.op("g", f)
        self.op("g", lambda e: e.memset(self.selq[64:128, 1:2], 1.0), self.c_g)
        g_const = self.c_g.n

        self.dmaS("y", self.ltriB[:], i["ltri_b"], self.d_msc)
        self.dmaS("y", self.ltriF[:], i["ltri_f"], self.d_msc)
        self.dmaS("y", self.wr_f[:],
                  self.ap(i["Wr"], 0, [[E, 128], [128 * E, NDT], [1, E]]), self.d_msc)
        if self.nfb:
            self.dmaS("y", self.m28_sb[:], i["m28"], self.d_msc)
        self.dma("g", self.cosr[:], i["cosrep"], self.d_cos)
        self.dma("g", self.sinr[:], i["sinrep"], self.d_cos)
        self.d_cos_n = self.d_cos.n
        d_consts = self.d_msc.n

        self.dma("g", self.scr["wos"], i["wo"], self.d_wos)
        self.dma("g", self.scr["W1s"][0], i["W1"][0], self.d_cv[0])
        self.dma("g", self.scr["W2s"][0], i["W2"][0], self.d_cv[0])
        self.conv_todo = list(range(1, E))

        self.wait("t", self.d_msc, d_consts)
        self.wait("t", self.c_g, g_const)
        for t_ in range(NDT):
            self.mm(PS[0:1, 0:E], self.ONECF[:], self.wr_f[:, E * t_: E * t_ + E],
                    t_ == 0, t_ == NDT - 1,
                    sem=(self.c_mm if t_ == NDT - 1 else None))
        self.wait("v", self.c_mm)
        self.op("v", lambda e: e.tensor_scalar(out=self.negcs_f[:], in0=PS[0:1, 0:E],
                                               scalar1=-1.0 / D, scalar2=None,
                                               op0=ALU.mult), self.c_v)
        self.wait("v", self.c_v)
        self.op("v", lambda e: e.tensor_copy(self.negcs_b[:], self.negcs_f[:]), self.c_v)
        self.op("v", lambda e: e.tensor_copy(self.wr_b[:], self.wr_f[:]), self.c_v)
        self.wait("v", self.c_v)
        self.op("v", lambda e: e.tensor_sub(self.wr_lo[:], self.wr_f[:], self.wr_b[:]),
                self.c_v)
        self.wait("t", self.c_v)
        self.mm(PS[:, 0:E], self.ONESF[0:1, 0:128], self.negcs_f[:], True, True,
                sem=self.c_mm)
        self.wait("v", self.c_mm)
        self.op("v", lambda e: e.tensor_copy(self.negcsb_t[:], PS[0:1, 0:E] if False
                                             else PS[:, 0:E]), self.c_v)
        self.w_negcs_v = self.c_v.n

        for e_ in range(E):
            for qtr in range(4):
                self.dmaS("y", self.rowstg[0:1, :],
                          self.ap(i["b1"], e_ * FF + 1024 * qtr, [[FF, 1], [1, 1024]]),
                          self.d_msc)
                self.wait("t", self.d_msc)
                for f8 in range(8):
                    ft = 8 * qtr + f8
                    self.mm(PS[:, 512 + f8: 512 + f8 + 1],
                            self.rowstg[0:1, 128 * f8: 128 * f8 + 128],
                            self.ONESF[0:1, 0:1], f8 == 0, f8 == 7,
                            sem=(self.c_mm if f8 == 7 else None))
                self.wait("v", self.c_mm)
                self.op("v", lambda e, ee=e_, q=qtr: e.tensor_copy(
                    self.b1c[:, 32 * ee + 8 * q: 32 * ee + 8 * q + 8],
                    PS[:, 512: 512 + 8]), self.c_v)
                self.wait("t", self.c_v)
                self.wait("y", self.c_v)

        def permrow(src_name, dst1, dst2, scale):
            self.dmaS("y", self.rowstg[0:1, 0:D], i[src_name], self.d_msc)
            self.wait("s", self.d_msc)
            ev = self.ap(self.rowstg, 0, [[D, 1], [64, 16], [2, 32]])
            od = self.ap(self.rowstg, 1, [[D, 1], [64, 16], [2, 32]])
            for (a, dst, off, sc) in ((ev, dst1, 0, scale), (od, dst1, 32, scale),
                                      (od, dst2, 0, -scale), (ev, dst2, 32, scale)):
                o = self.ap(dst, off, [[D, 1], [64, 16], [1, 32]])
                self.op("s", lambda e, a=a, o=o, k=sc:
                        e.activation(o, a, AF.Copy, scale=k), self.c_s)
            self.wait("y", self.c_s)

        permrow("bq", self.bq1r, self.bq2r, QSC)
        permrow("bk", self.bk1r, self.bk2r, QSC)
        for src, dst in (("bo", self.bo_r), ("bv", self.bv_r)):
            self.dmaS("y", self.rowstg[0:1, 0:D], i[src], self.d_msc)
            self.wait("s", self.d_msc)
            self.op("s", lambda e, d=dst: e.copy(d[0:1, :], self.rowstg[0:1, 0:D]),
                    self.c_s)
            self.wait("y", self.c_s)

        for t_ in range(NDT):
            for (wname, s1, s2, sc) in (("wq", "wq1s", "wq2s", QSC),
                                        ("wk", "wk1s", "wk2s", QSC)):
                for h2 in range(2):
                    self.dmaS("y", self.stage[:],
                              i[wname][128 * t_: 128 * t_ + 128,
                                       512 * h2: 512 * h2 + 512], self.d_msc)
                    self.wait("s", self.d_msc)
                    ev = self.ap(self.stage, 0, [[512, 128], [64, 8], [2, 32]])
                    od = self.ap(self.stage, 1, [[512, 128], [64, 8], [2, 32]])
                    lo = self.ap(self.stage_b, 0, [[512, 128], [64, 8], [1, 32]])
                    hi = self.ap(self.stage_b, 32, [[512, 128], [64, 8], [1, 32]])
                    self.op("s", lambda e, a=ev, o=lo, k=sc:
                            e.activation(o, a, AF.Copy, scale=k))
                    self.op("s", lambda e, a=od, o=hi, k=sc:
                            e.activation(o, a, AF.Copy, scale=k), self.c_s)
                    self.wait("y", self.c_s)
                    self.dmaS("y", self.scr[s1][128 * t_: 128 * t_ + 128,
                                                512 * h2: 512 * h2 + 512],
                              self.stage_b[:], self.d_msc)
                    self.wait("s", self.d_msc)
                    self.op("s", lambda e, a=od, o=lo, k=sc:
                            e.activation(o, a, AF.Copy, scale=-k))
                    self.op("s", lambda e, a=ev, o=hi, k=sc:
                            e.activation(o, a, AF.Copy, scale=k), self.c_s)
                    self.wait("y", self.c_s)
                    self.dmaS("y", self.scr[s2][128 * t_: 128 * t_ + 128,
                                                512 * h2: 512 * h2 + 512],
                              self.stage_b[:], self.d_msc)
                    self.wait("s", self.d_msc)
        self.d_wscr = self.d_msc.n

    def phase_F(self):
        PS = self.PS
        if self.nfb == 0:
            for e_c in self.conv_todo:
                self.dma("g", self.scr["W1s"][e_c], self.i["W1"][e_c], self.d_cv[e_c])
                self.dma("g", self.scr["W2s"][e_c], self.i["W2"][e_c], self.d_cv[e_c])
            self.conv_todo = []
            self.f_mm_end = self.c_mm.n
            self.f_v_end = self.c_v.n
            return
        i = self.i
        bufs = [self.xtb, self.att]
        nchunks = self.nfb * NCH
        issued = []

        def issue(k):
            fb, c_ = divmod(k, NCH)
            self.wait("g", self.c_mm, self.xtb_guard[k % 2])
            self.wait("g", self.c_s, self.xtb_guard_s[k % 2])
            self.wait("g", self.c_v, self.xtb_guard_v[k % 2])
            self.dma("g", bufs[k % 2][:],
                     self.ap(i["xT_for"], fb * D * S + 512 * c_,
                             [[S, 128], [128 * S, NDT], [1, 512]]), self.d_xt[k % 2])
            issued.append(self.d_xt[k % 2].n)

        issue(0)
        issue(1)
        for k in range(nchunks):
            fb, c_ = divmod(k, NCH)
            buf = bufs[k % 2]
            self.wait("t", self.d_xt[k % 2], issued[k])
            for t_ in range(NDT):
                self.mm(PS[0:1, 3584:4096], self.ONECB[:],
                        buf[:, 512 * t_: 512 * t_ + 512], t_ == 0, t_ == NDT - 1,
                        sem=(self.c_mm if t_ == NDT - 1 else None))
            self.wait("s", self.c_mm)
            self.op("s", lambda e: e.copy(self.mrow_b[:], PS[0:1, 3584:4096]),
                    self.c_s)
            self.wait("t", self.c_s)
            for j4 in range(4):
                for t_ in range(NDT):
                    self.mm(PS[:, 3584:3592],
                            buf[:, 512 * t_ + 128 * j4: 512 * t_ + 128 * j4 + 128],
                            self.wr_b[:, E * t_: E * t_ + E], t_ == 0, False)
                self.mm(PS[:, 3584:3592], self.mrow_b[0:1, 128 * j4: 128 * j4 + 128],
                        self.negcs_b[:], False, True, sem=self.c_mm)
                self.wait("v", self.c_mm)
                self.op("v", lambda e: e.tensor_copy(self.s_sb[:], PS[:, 3584:3592]),
                        self.c_v)
                self.wait("t", self.c_v)
                self.wait("v", self.c_v)
                self.op("v", lambda e: e.max(out=self.top_sb[:], in_=self.s_sb[:]),
                        self.c_v)
                self.wait("v", self.c_v)
                self.op("v", lambda e: e.tensor_scalar(
                    out=self.m1_sb[:], in0=self.s_sb[:], scalar1=self.top_sb[:, 0:1],
                    scalar2=None, op0=ALU.is_equal), self.c_v)
                self.op("v", lambda e: e.tensor_scalar(
                    out=self.c1_sb[:], in0=self.s_sb[:], scalar1=self.top_sb[:, 1:2],
                    scalar2=None, op0=ALU.is_equal), self.c_v)
                self.wait("v", self.c_v)
                self.op("v", lambda e: e.tensor_add(
                    self.ohacc[:, 0:8], self.ohacc[:, 0:8], self.m1_sb[:]), self.c_v)
                self.wait("v", self.c_v)
                self.op("v", lambda e: e.tensor_add(
                    self.ohacc[:, 8:16], self.ohacc[:, 8:16], self.c1_sb[:]), self.c_v)
                self.wait("v", self.c_v)
            self.xtb_guard[k % 2] = self.c_mm.n
            if k % 14 == 13 and self.conv_todo:
                e_c = self.conv_todo.pop(0)
                self.dma("g", self.scr["W1s"][e_c], i["W1"][e_c], self.d_cv[e_c])
                self.dma("g", self.scr["W2s"][e_c], i["W2"][e_c], self.d_cv[e_c])
            if k + 2 < nchunks:
                issue(k + 2)
            if c_ == NCH - 1:
                self.wait("t", self.c_v)
                self.mm(PS[0:1, 3600:3616], self.ONECF[:], self.ohacc[:], True, True,
                        sem=self.c_mm)
                self.wait("v", self.c_mm)
                self.op("v", lambda e, fb=fb: e.scalar_tensor_tensor(
                    out=self.baseoff[:], in0=PS[0:1, 3600:3616],
                    scalar=self.m28_sb[0:1, fb:fb + 1], in1=self.baseoff[:],
                    op0=ALU.mult, op1=ALU.add), self.c_v)
                self.wait("v", self.c_v)
                self.op("v", lambda e: e.memset(self.ohacc[:], 0.0), self.c_v)
                self.wait("t", self.c_v)
        for e_c in self.conv_todo:
            self.dma("g", self.scr["W1s"][e_c], i["W1"][e_c], self.d_cv[e_c])
            self.dma("g", self.scr["W2s"][e_c], i["W2"][e_c], self.d_cv[e_c])
        self.conv_todo = []
        self.mm(PS[:, 3616:3632], self.ONESF[0:1, 0:128], self.baseoff[:], True, True,
                sem=self.c_mm)
        self.wait("v", self.c_mm)
        self.op("v", lambda e: e.tensor_copy(self.offb[:], PS[:, 3616:3632]), self.c_v)
        self.f_mm_end = self.c_mm.n
        self.f_v_end = self.c_v.n

    def phase_R(self, b):
        self.wait("t", self.c_v)
        if b == 0 and "dbg_wrb_early" in self.taps:
            self.wait("s", self.c_v)
            self.dma("s", self.taps["dbg_wrb_early"], self.wr_b[:], self.d_tap)
        PS = self.PS
        i = self.i
        SX = PS[0:1, 2048:2560]
        SQ = PS[0:1, 2560:3072]
        SRT = PS[:, 3072:3080]
        bufs = [self.xtb, self.att]
        issued = {}

        def issue(c_):
            self.wait("g", self.c_mm, self.xtb_guard[c_ % 2])
            self.wait("g", self.c_s, self.xtb_guard_s[c_ % 2])
            self.wait("g", self.c_v, self.xtb_guard_v[c_ % 2])
            self.dma("g", bufs[c_ % 2][:],
                     self.ap(i["xT_own"], b * D * S + 512 * c_,
                             [[S, 128], [128 * S, NDT], [1, 512]]), self.d_xt[c_ % 2])
            issued[c_] = self.d_xt[c_ % 2].n

        issue(0)
        issue(1)
        srt_free = 0
        for c_ in range(NCH):
            buf = bufs[c_ % 2]
            self.wait("t", self.d_xt[c_ % 2], issued[c_])
            self.wait("s", self.d_xt[c_ % 2], issued[c_])
            for t_ in range(NDT):
                self.mm(SX, self.ONECB[:], buf[:, 512 * t_: 512 * t_ + 512],
                        t_ == 0, t_ == NDT - 1,
                        sem=(self.c_mm if t_ == NDT - 1 else None))
            for t_ in range(NDT):
                self.wait("s", self.c_mm)
                self.op("s", lambda e, t=t_, bu=buf: e.activation(
                    self.xsq[:], bu[:, 512 * t: 512 * t + 512], AF.Square), self.c_s)
                self.wait("t", self.c_s)
                self.mm(SQ, self.ONECB[:], self.xsq[:], t_ == 0, t_ == NDT - 1,
                        sem=self.c_mm)
            sq_mm = self.c_mm.n
            self.wait("v", self.c_mm, sq_mm)
            self.op("v", lambda e: e.tensor_copy(self.sxrow[:], SX), self.c_v)
            self.op("v", lambda e: e.tensor_scalar(
                out=self.mrow_t[:], in0=SX, scalar1=1.0 / D, scalar2=None,
                op0=ALU.mult), self.c_v)
            self.op("v", lambda e: e.tensor_scalar(
                out=self.var_t[:], in0=SQ, scalar1=1.0 / D, scalar2=None,
                op0=ALU.mult), self.c_v)
            self.wait("v", self.c_v)
            self.wait("t", self.c_v)
            self.op("v", lambda e: e.tensor_mul(self.rstd_f[:], self.mrow_t[:],
                                                self.mrow_t[:]), self.c_v)
            self.wait("v", self.c_v)
            self.op("v", lambda e: e.tensor_sub(self.var_t[:], self.var_t[:],
                                                self.rstd_f[:]), self.c_v)
            self.wait("s", self.c_v)
            self.op("s", lambda e: e.activation(self.rstd_f[:], self.var_t[:],
                                                AF.Ln,
                                                bias=self.smalls[0:1, 6:7]),
                    self.c_s)
            self.wait("s", self.c_s)
            self.op("s", lambda e: e.activation(self.rstd_f[:], self.rstd_f[:],
                                                AF.Exp, scale=-0.5), self.c_s)
            rstd_done = self.c_s.n
            self.wait("v", self.c_s, rstd_done)
            self.op("v", lambda e, c=c_: e.tensor_copy(
                self.rstd_bf[0:1, 512 * c: 512 * c + 512], self.rstd_f[:]), self.c_v)
            self.op("v", lambda e: e.tensor_mul(self.mrow_t[:], self.mrow_t[:],
                                                self.rstd_f[:]), self.c_v)
            self.wait("v", self.c_v)
            self.op("v", lambda e, c=c_: e.tensor_copy(
                self.mrstd_bf[0:1, 512 * c: 512 * c + 512], self.mrow_t[:]), self.c_v)
            self.wait("t", self.c_s, rstd_done)
            for j4 in range(4):
                self.mm(PS[:, 2560 + j4: 2561 + j4],
                        self.rstd_f[0:1, 128 * j4: 128 * j4 + 128],
                        self.ONESF[0:1, 0:1], j4 == 0, j4 == 3,
                        sem=(self.c_mm if j4 == 3 else None))
            self.wait("v", self.c_mm)
            self.op("v", lambda e, c=c_: e.tensor_copy(
                self.rstd_cols[:, 4 * c: 4 * c + 4], PS[:, 2560:2564]), self.c_v)
            rcols_v = self.c_v.n

            for j4 in range(4):
                j = 4 * c_ + j4
                self.wait("y", self.c_mm, srt_free)
                self.wait("y", self.c_v, getattr(self, "_xlo_v", 0))
                self.dma("y", self.xt32[:],
                         self.ap(i["xT_own"], b * D * S + 512 * c_ + 128 * j4,
                                 [[S, 128], [128 * S, NDT], [1, 128]]), self.d_x32)
                self.wait("v", self.d_x32)
                xhi = self.ap(buf, 128 * j4, [[NDT * 512, 128], [512, NDT], [1, 128]])
                self.op("v", lambda e, a=xhi: e.tensor_sub(
                    self.xlo[:], self.xt32[:], a), self.c_v)
                self._xlo_v = self.c_v.n
                self.wait("t", self.c_v, self._xlo_v)
                for t_ in range(NDT):
                    self.mm(SRT,
                            self.ap(buf, 512 * t_ + 128 * j4,
                                    [[NDT * 512, 128], [1, 128]]),
                            self.wr_b[:, E * t_: E * t_ + E], t_ == 0, False)
                for t_ in range(NDT):
                    self.mm(SRT,
                            self.ap(buf, 512 * t_ + 128 * j4,
                                    [[NDT * 512, 128], [1, 128]]),
                            self.wr_lo[:, E * t_: E * t_ + E], False, False)
                for t_ in range(NDT - 1):
                    self.mm(SRT, self.xlo[:, 128 * t_: 128 * t_ + 128],
                            self.wr_b[:, E * t_: E * t_ + E], False, False)
                self.mm(SRT, self.xlo[:, 128 * (NDT - 1): 128 * NDT],
                        self.wr_b[:, E * (NDT - 1): E * NDT], False, False)
                self.mm(PS[:, 3080:3081],
                        self.sxrow[0:1, 128 * j4: 128 * j4 + 128],
                        self.ONESF[0:1, 0:1], True, True, sem=self.c_mm)
                srt_free = self.c_mm.n
                self.wait("v", self.c_mm, srt_free)
                self.op("v", lambda e: e.tensor_copy(self.colB[:], PS[:, 3080:3081]),
                        self.c_v)
                self.wait("v", self.c_v)
                self.op("v", lambda e: e.scalar_tensor_tensor(
                    out=self.s_sb[:], in0=self.negcsb_t[:], scalar=self.colB[:],
                    in1=SRT, op0=ALU.mult, op1=ALU.add), self.c_v)
                self.wait("t", self.c_v)
                self.wait("v", self.c_v)
                if b == 0 and c_ == 0 and j4 == 0 and "dbg_s" in self.taps:
                    self.op("v", lambda e: e.tensor_copy(self.keep_t[:, 0:8], SRT),
                            self.c_v)
                    self.wait("s", self.c_v)
                    self.dma("s", self.taps["dbg_s"], self.s_sb[:], self.d_tap)
                    self.dma("s", self.taps["dbg_srt"], self.keep_t[:, 0:8], self.d_tap)
                    self.dma("s", self.taps["dbg_colB"], self.colB[:], self.d_tap)
                    self.dma("s", self.taps["dbg_xlo"], self.xlo[:], self.d_tap)
                    self.dma("s", self.taps["dbg_xtb"], self.xtb[:], self.d_tap)
                    self.dma("s", self.taps["dbg_negcsb"], self.negcsb_t[:], self.d_tap)
                    self.dma("s", self.taps["dbg_wrf"], self.wr_f[:], self.d_tap)
                    self.dma("s", self.taps["dbg_wrb"], self.wr_b[:], self.d_tap)
                    self.dma("s", self.taps["dbg_wrlo"], self.wr_lo[:], self.d_tap)
                self.op("v", lambda e: e.max(out=self.top_sb[:], in_=self.s_sb[:]),
                        self.c_v)
                self.wait("v", self.c_v)
                self.op("v", lambda e: e.tensor_scalar(
                    out=self.m1_sb[:], in0=self.s_sb[:], scalar1=self.top_sb[:, 0:1],
                    scalar2=None, op0=ALU.is_equal), self.c_v)
                self.wait("v", self.c_v)
                self.op("v", lambda e: e.tensor_tensor_scan(
                    out=self.c1_sb[:], data0=self.m1_sb[:], data1=self.zer8[:],
                    initial=0.0, op0=ALU.add, op1=ALU.bypass), self.c_v)
                self.wait("v", self.c_v)
                self.op("v", lambda e: e.tensor_scalar(
                    out=self.c1_sb[:], in0=self.c1_sb[:], scalar1=1.0, scalar2=None,
                    op0=ALU.is_equal), self.c_v)
                self.wait("v", self.c_v)
                self.op("v", lambda e, j=j: e.tensor_mul(
                    self.oh1_sb[:, E * j: E * j + E], self.m1_sb[:], self.c1_sb[:]),
                    self.c_v)
                self.op("v", lambda e: e.tensor_scalar(
                    out=self.colA[:], in0=self.top_sb[:, 1:2],
                    scalar1=self.top_sb[:, 0:1], scalar2=1.0,
                    op0=ALU.is_equal, op1=ALU.add), self.c_v)
                self.op("v", lambda e: e.tensor_scalar(
                    out=self.m1_sb[:], in0=self.s_sb[:], scalar1=self.top_sb[:, 1:2],
                    scalar2=None, op0=ALU.is_equal), self.c_v)
                self.wait("v", self.c_v)
                self.op("v", lambda e: e.tensor_tensor_scan(
                    out=self.c1_sb[:], data0=self.m1_sb[:], data1=self.zer8[:],
                    initial=0.0, op0=ALU.add, op1=ALU.bypass), self.c_v)
                self.wait("v", self.c_v)
                self.op("v", lambda e: e.tensor_scalar(
                    out=self.c1_sb[:], in0=self.c1_sb[:], scalar1=self.colA[:],
                    scalar2=None, op0=ALU.is_equal), self.c_v)
                self.wait("v", self.c_v)
                self.op("v", lambda e, j=j: e.tensor_mul(
                    self.oh2_sb[:, E * j: E * j + E], self.m1_sb[:], self.c1_sb[:]),
                    self.c_v)
                oh_done = self.c_v.n
                self.wait("v", self.c_v, rcols_v)
                self.op("v", lambda e: e.tensor_sub(self.colA[:], self.top_sb[:, 0:1],
                                                    self.top_sb[:, 1:2]), self.c_v)
                self.wait("v", self.c_v)
                self.op("v", lambda e, j=j: e.tensor_mul(
                    self.colA[:], self.colA[:], self.rstd_cols[:, j: j + 1]), self.c_v)
                self.wait("s", self.c_v)
                self.op("s", lambda e: e.activation(self.colB[:], self.colA[:],
                                                    AF.Exp, scale=-1.0), self.c_s)
                self.wait("v", self.c_s)
                self.op("v", lambda e: e.tensor_scalar(
                    out=self.colA[:], in0=self.colB[:], scalar1=1.0, scalar2=None,
                    op0=ALU.add), self.c_v)
                self.wait("v", self.c_v)
                self.op("v", lambda e, j=j: e.reciprocal(
                    self.w1c[:, j: j + 1], self.colA[:]), self.c_v)
                self.wait("v", self.c_v)
                self.op("v", lambda e, j=j: e.tensor_mul(
                    self.w2c[:, j: j + 1], self.colB[:], self.w1c[:, j: j + 1]),
                    self.c_v)
                self.wait("t", self.c_v, oh_done)
                self.mm(PS[:, 3104:3112], self.ltriB[:],
                        self.oh1_sb[:, E * j: E * j + E], True, False)
                self.mm(PS[:, 3112:3120], self.ltriB[:],
                        self.oh2_sb[:, E * j: E * j + E], False, False)
                self.mm(PS[:, 3104:3120], self.ONESF[0:1, 0:128], self.runoff[:],
                        False, True, sem=self.c_mm)
                self.wait("v", self.c_mm)
                self.op("v", lambda e, j=j: e.tensor_copy(
                    self.rank_sb[:, 16 * j: 16 * j + 16], PS[:, 3104:3120]), self.c_v)
                self.wait("t", self.c_v)
                self.mm(PS[0:1, 3136:3144], self.ONECB[:],
                        self.oh1_sb[:, E * j: E * j + E], True, False)
                self.mm(PS[0:1, 3144:3152], self.ONECB[:],
                        self.oh2_sb[:, E * j: E * j + E], False, True, sem=self.c_mm)
                self.wait("v", self.c_mm)
                self.op("v", lambda e: e.tensor_add(self.runoff[:], self.runoff[:],
                                                    PS[0:1, 3136:3152]), self.c_v)
                self.wait("t", self.c_v)
            self.xtb_guard[c_ % 2] = self.c_mm.n
            self.xtb_guard_s[c_ % 2] = self.c_s.n
            self.xtb_guard_v[c_ % 2] = self.c_v.n
            if c_ + 2 < NCH:
                issue(c_ + 2)

        for c_ in range(NCH):
            buf = bufs[c_ % 2]
            self.wait("g", self.c_mm, self.xtb_guard[c_ % 2])
            self.wait("g", self.c_s, self.xtb_guard_s[c_ % 2])
            self.wait("g", self.c_v)
            self.dma("g", buf[:],
                     self.ap(i["xT_own"], b * D * S + 512 * c_,
                             [[S, 128], [128 * S, NDT], [1, 512]]), self.d_xt[c_ % 2])
            self.wait("t", self.d_xt[c_ % 2])
            self.wait("t", self.c_v)
            if b == 0 and c_ == 0:
                self.wait("t", self.c_v, self.w_negcs_v)
            self.mm(PS[:, 0:512], self.ONESB[0:1, 0:128],
                    self.rstd_bf[0:1, 512 * c_: 512 * c_ + 512], True, True)
            self.mm(PS[:, 512:1024], self.ONESB[0:1, 0:128],
                    self.mrstd_bf[0:1, 512 * c_: 512 * c_ + 512], True, True,
                    sem=self.c_mm)
            self.wait("v", self.c_mm)
            self.wait("v", self.d_xt[c_ % 2])
            if self.m_h_mm_end:
                self.wait("v", self.c_mm, self.m_h_mm_end)
            for t_ in range(NDT):
                self.op("v", lambda e, t=t_, bu=buf: e.tensor_mul(
                    self.tmpa[:], bu[:, 512 * t: 512 * t + 512], PS[:, 0:512]),
                    self.c_v)
                self.wait("v", self.c_v)
                self.op("v", lambda e, t=t_, c=c_: e.tensor_sub(
                    self.xnT[:, S * t + 512 * c: S * t + 512 * c + 512],
                    self.tmpa[:], PS[:, 512:1024]), self.c_v)
                self.wait("v", self.c_v)
            self.xtb_guard[c_ % 2] = self.c_mm.n
            self.xtb_guard_v[c_ % 2] = self.c_v.n
        self.r_xnT_v = self.c_v.n
        self.r_mm_end = self.c_mm.n

    def phase_A(self, b):
        PS = self.PS
        i = self.i
        qT = self.qTp
        kT = self.kTp
        vS = self.v_sb
        qk_free = 0
        v_free = 0
        wvp_free = 0
        att_sq_free = self.r_mm_end
        att_sq_free_s = 0
        slot_free = {}
        self.wait("t", self.c_v, self.r_xnT_v)
        if self.m_y_mm_end:
            self.wait("v", self.c_mm, self.m_y_mm_end)
            self.wait("s", self.c_mm, self.m_y_mm_end)
        self.wait("s", self.c_mm, self.r_mm_end)
        self.wait("v", self.c_mm, self.r_mm_end)
        if b == 0:
            self.wait("y", self.d_msc, self.d_wscr)
            self.wait("v", self.d_cos, self.d_cos_n)

        for p in range(NPAIR):
            self.wait("g", self.c_mm, wvp_free)
            self.dma("g", self.wvp[:],
                     self.ap(i["wv"], 128 * p, [[D, 128], [128 * D, NDT], [1, 128]]),
                     self.d_wv)
            wv_done = self.d_wv.n
            self.wait("t", self.c_mm, qk_free)
            self.wait("v", self.c_mm, qk_free)
            wqp_guard = qk_free
            for (dst, snames, br1, br2) in (
                    (qT, ("wq1s", "wq2s"), self.bq1r, self.bq2r),
                    (kT, ("wk1s", "wk2s"), self.bk1r, self.bk2r)):
                self.wait("y", self.c_mm, wqp_guard)
                for w_i, sname in enumerate(snames):
                    self.dma("y", self.wqp[:, D * w_i: D * w_i + D],
                             self.ap(self.scr[sname], 128 * p,
                                     [[D, 128], [128 * D, NDT], [1, 128]]), self.d_wa)
                self.wait("t", self.d_wa)
                for c4 in range(4):
                    for t_ in range(NDT):
                        self.mm(PS[:, 3072:3584],
                                self.wqp[:, 128 * t_: 128 * t_ + 128],
                                self.xnT[:, S * t_ + 512 * c4: S * t_ + 512 * c4 + 512],
                                t_ == 0, False)
                    self.mm(PS[:, 3072:3584], br1[0:1, 128 * p: 128 * p + 128],
                            self.ONESB[0:1, 0:512], False, True)
                    for t_ in range(NDT):
                        self.mm(PS[:, 3584:4096],
                                self.wqp[:, D + 128 * t_: D + 128 * t_ + 128],
                                self.xnT[:, S * t_ + 512 * c4: S * t_ + 512 * c4 + 512],
                                t_ == 0, False)
                    self.mm(PS[:, 3584:4096], br2[0:1, 128 * p: 128 * p + 128],
                            self.ONESB[0:1, 0:512], False, True, sem=self.c_mm)
                    self.wait("v", self.c_mm)
                    self.op("v", lambda e, c=c4: e.tensor_mul(
                        self.tmpa[:], PS[:, 3072:3584],
                        self.cosr[:, 512 * c: 512 * c + 512]), self.c_v)
                    self.op("v", lambda e, c=c4: e.tensor_mul(
                        self.xsq[:], PS[:, 3584:4096],
                        self.sinr[:, 512 * c: 512 * c + 512]), self.c_v)
                    self.wait("v", self.c_v)
                    self.op("v", lambda e, c=c4, d=dst: e.tensor_add(
                        d[:, 512 * c: 512 * c + 512], self.tmpa[:], self.xsq[:]),
                        self.c_v)
                    self.wait("t", self.c_v)
                wqp_guard = self.c_mm.n
            rope_v = self.c_v.n
            self.wait("g", self.c_mm, v_free)
            zap = self.ap(vS, 64, [[130 * NTOKM, 128], [130, NTOKM], [65, 2], [1, 1]])
            self.op("g", lambda e, a=zap: e.memset(a, 1.0), self.c_g)
            vz = self.c_g.n
            self.wait("v", self.c_mm, v_free)
            self.wait("v", self.c_g, vz)
            self.wait("t", self.d_wv, wv_done)
            for j in range(NTOKM):
                for t_ in range(NDT):
                    self.mm(PS[:, 3072:3200],
                            self.xnT[:, S * t_ + 128 * j: S * t_ + 128 * j + 128],
                            self.wvp[:, 128 * t_: 128 * t_ + 128], t_ == 0, False)
                self.mm(PS[:, 3072:3200], self.ONESB[0:1, 0:128],
                        self.bv_r[0:1, 128 * p: 128 * p + 128], False, True,
                        sem=self.c_mm)
                self.wait("v", self.c_mm)
                oap = self.ap(vS, 130 * j, [[130 * NTOKM, 128], [65, 2], [1, 64]])
                self.op("v", lambda e, o=oap: e.tensor_copy(o, PS[:, 3072:3200]),
                        self.c_v)
                self.wait("t", self.c_v)
            self.wait("v", self.c_mm, att_sq_free)
            if att_sq_free_s:
                self.wait("v", self.c_s, att_sq_free_s)
            for (src, col) in ((qT, 0), (kT, 1)):
                self.op("v", lambda e, sr=src: e.tensor_mul(
                    self.att[:, 0:2048], sr[:], sr[:]), self.c_v)
                self.wait("t", self.c_v)
                for c4 in range(4):
                    self.mm(PS[0:2, 3584:4096], self.selq[:],
                            self.att[:, 512 * c4: 512 * c4 + 512], True, True,
                            sem=self.c_mm)
                    self.wait("v", self.c_mm)
                    self.op("v", lambda e, c=c4: e.reduce_max(
                        self.smalls[0:2, 8 + c: 9 + c], PS[0:2, 3584:4096],
                        axis=AX.X), self.c_v)
                    self.wait("t", self.c_v)
                self.wait("v", self.c_v)
                self.op("v", lambda e, cl=col: e.reduce_max(
                    self.smalls[0:2, cl: cl + 1], self.smalls[0:2, 8:12], axis=AX.X),
                    self.c_v)
                self.wait("v", self.c_v)
            self.op("v", lambda e: e.tensor_add(self.smalls[0:2, 2:3],
                                                self.smalls[0:2, 0:1],
                                                self.smalls[0:2, 1:2]), self.c_v)
            self.wait("t", self.c_v)
            for h in range(2):
                self.mm(PS[0:1, 3584 + h: 3585 + h],
                        self.smalls[h:h + 1, 2:3] if h == 0 else None,
                        self.ONESF[0:1, 0:1], h == 0, h == 1, sem=None) \
                    if False else None
            self.mm(PS[0:1, 3584:3586], self.smalls[0:2, 2:3], self.ltriF[0:2, 0:2],
                    True, True, sem=self.c_mm)
            self.wait("s", self.c_mm)
            self.op("s", lambda e: e.activation(self.negu_r[0:1, 0:1],
                                                PS[0:1, 3584:3585], AF.Copy,
                                                scale=-0.5), self.c_s)
            self.wait("s", self.c_s)
            self.op("s", lambda e: e.activation(self.colA[0:1, 0:1],
                                                PS[0:1, 3585:3586], AF.Copy,
                                                scale=-0.5), self.c_s)
            self.wait("v", self.c_s)
            self.op("v", lambda e: e.tensor_sub(self.negu_r[0:1, 1:2],
                                                self.colA[0:1, 0:1],
                                                self.negu_r[0:1, 0:1]), self.c_v)
            self.wait("t", self.c_v)
            for h in range(2):
                self.mm(PS[:, 3586 + h: 3587 + h], self.ONESB[0:1, 0:128],
                        self.negu_r[0:1, h: h + 1], True, True,
                        sem=(self.c_mm if h == 1 else None))
            self.wait("v", self.c_mm)
            self.op("v", lambda e: e.tensor_copy(self.smalls[:, 4:6],
                                                 PS[:, 3586:3588]), self.c_v)
            negu_v = self.c_v.n

            exp_last = {0: 0, 1: 0}
            bank0_v_free = 0
            for Qb in range(4):
                njt = 4 * Qb + 4
                for J in range(njt):
                    par = J % 2
                    for h in range(2):
                        bk = par if h == 0 else 2 + par
                        if exp_last[h]:
                            self.wait("t", self.c_s, exp_last[h])
                        if h == 0 and bank0_v_free:
                            self.wait("t", self.c_v, bank0_v_free)
                        self.mm(PS[:, 512 * bk: 512 * bk + 512],
                                kT[64 * h: 64 * h + 64, 128 * J: 128 * J + 128],
                                qT[64 * h: 64 * h + 64, 512 * Qb: 512 * Qb + 512],
                                True, True, tp=(64 * h, 0),
                                sem=(self.c_mm if h == 1 else None))
                    sc_mm = self.c_mm.n
                    if par == 1:
                        J0 = J - 1
                        sl = (J0 // 2) % 2
                        for h in range(2):
                            bk0 = 0 if h == 0 else 2
                            self.wait("s", self.c_mm, sc_mm)
                            self.wait("s", self.c_v, negu_v)
                            gd = slot_free.get((h, sl), 0)
                            if gd:
                                self.wait("s", self.c_mm, gd)
                            self.op("s", lambda e, h=h, b0=bk0, sl=sl: e.activation(
                                self.att[:, 2048 * h + 1024 * sl:
                                         2048 * h + 1024 * sl + 1024],
                                PS[:, 512 * b0: 512 * b0 + 1024], AF.Exp,
                                bias=self.smalls[:, 4 + h: 5 + h]), self.c_s)
                            exp_last[h] = self.c_s.n
                            mask_v = 0
                            for JJ in (J0, J):
                                m = JJ - 4 * Qb
                                if m >= 0:
                                    off = 1024 * sl + 512 * (JJ - J0)
                                    self.wait("v", self.c_s, exp_last[h])
                                    self.op("v", lambda e, h=h, o=off, m=m:
                                            e.tensor_mul(
                                        self.att[:, 2048 * h + o + 128 * m:
                                                 2048 * h + o + 128 * m + 128],
                                        self.att[:, 2048 * h + o + 128 * m:
                                                 2048 * h + o + 128 * m + 128],
                                        self.ltriB[:]), self.c_v)
                                    mask_v = self.c_v.n
                            self.wait("t", self.c_s, exp_last[h])
                            if mask_v:
                                self.wait("t", self.c_v, mask_v)
                            for JJ in (J0, J):
                                m = JJ - 4 * Qb
                                lo = max(0, 128 * m)
                                off = 1024 * sl + 512 * (JJ - J0)
                                self.mm(PS[0:65, 512 * (4 + h) + lo:
                                           512 * (4 + h) + 512],
                                        self.ap(vS, 130 * JJ + 65 * h,
                                                [[130 * NTOKM, 128], [1, 65]]),
                                        self.att[:, 2048 * h + off + lo:
                                                 2048 * h + off + 512],
                                        JJ == 0, JJ == njt - 1,
                                        sem=(self.c_mm if JJ == J else None))
                            slot_free[(h, sl)] = self.c_mm.n
                av_mm = self.c_mm.n
                self.wait("v", self.c_mm, av_mm)
                for h in range(2):
                    self.op("v", lambda e, h=h: e.reciprocal(
                        self.zrec[0:1, :] if h == 0 else self.var_t[0:1, :],
                        PS[64:65, 512 * (4 + h): 512 * (4 + h) + 512]), self.c_v)
                self.wait("v", self.c_v)
                self.op("v", lambda e: e.tensor_copy(self.zrec_b[0:1, :],
                                                     self.zrec[0:1, :]), self.c_v)
                self.op("v", lambda e: e.tensor_copy(self.mrstd_bf[0:1, 0:512],
                                                     self.var_t[0:1, :]), self.c_v)
                zb = self.c_v.n
                self.wait("t", self.c_v, zb)
                self.wait("t", self.c_s, exp_last[0])
                self.wait("t", self.c_s, exp_last[1])
                if bank0_v_free:
                    self.wait("t", self.c_v, bank0_v_free)
                self.mm(PS[0:64, 0:512], self.ONESB[0:1, 0:64],
                        self.zrec_b[0:1, :], True, True)
                self.mm(PS[0:64, 512:1024], self.ONESB[0:1, 0:64],
                        self.mrstd_bf[0:1, 0:512], True, True, sem=self.c_mm)
                zmm = self.c_mm.n
                self.wait("s", self.c_mm, zmm)
                self.wait("s", self.c_v)
                for h in range(2):
                    self.op("s", lambda e, h=h: e.copy(
                        self.aun[0:64, :] if h == 0 else self.tmpa[0:64, :],
                        PS[0:64, 512 * (4 + h): 512 * (4 + h) + 512]), self.c_s)
                self.wait("v", self.c_s)
                for h in range(2):
                    hg = 2 * p + h
                    self.op("v", lambda e, hg=hg, Qb=Qb, h=h: e.tensor_mul(
                        self.AH[64 * (hg % 2): 64 * (hg % 2) + 64,
                                S * (hg // 2) + 512 * Qb:
                                S * (hg // 2) + 512 * Qb + 512],
                        self.aun[0:64, :] if h == 0 else self.tmpa[0:64, :],
                        PS[0:64, 512 * h: 512 * h + 512]), self.c_v)
                bank0_v_free = self.c_v.n
                self.wait("t", self.c_v)
            last_pair = self.c_mm.n
            qk_free = last_pair
            v_free = last_pair
            wvp_free = last_pair
            att_sq_free = last_pair
            att_sq_free_s = self.c_s.n
        self.a_heads_v = self.c_v.n
        self.xtb_guard[1] = self.c_mm.n
        self.xtb_guard_s[1] = self.c_s.n
        self.xtb_guard_v[1] = self.c_v.n

        wo_free = [0, 0]
        self.wait("t", self.d_wos, 16)
        op_last_v = 0
        for g4 in range(4):
            for dv in range(NDT):
                wb = self.wop
                self.wait("y", self.c_mm, wo_free[0])
                self.dma("y", wb[:],
                         self.ap(self.scr["wos"], 128 * dv * D, [[D, 128], [1, D]]),
                         self.d_wo)
                self.wait("t", self.d_wo)
                if dv == 0:
                    self.wait("t", self.c_v, self.a_heads_v)
                    if op_last_v:
                        self.wait("t", self.c_v, op_last_v)
                for jj_i in range(4):
                    jj = 4 * g4 + jj_i
                    for half in range(2):
                        self.mm(PS[:, 1024 * jj_i + 512 * half:
                                   1024 * jj_i + 512 * half + 512],
                                self.AH[:, S * dv + 128 * jj: S * dv + 128 * jj + 128],
                                wb[:, 512 * half: 512 * half + 512],
                                dv == 0, False,
                                sem=(self.c_mm if (jj_i == 3 and half == 1)
                                     else None))
                wo_free[0] = self.c_mm.n
            for jj_i in range(4):
                for half in range(2):
                    self.mm(PS[:, 1024 * jj_i + 512 * half:
                               1024 * jj_i + 512 * half + 512],
                            self.ONESB[0:1, 0:128],
                            self.bo_r[0:1, 512 * half: 512 * half + 512],
                            False, True,
                            sem=(self.c_mm if (jj_i == 3 and half == 1) else None))
            op_mm = self.c_mm.n
            self.wait("v", self.c_mm, op_mm)
            if g4 == 0 and self.o_prev_v:
                self.wait("v", self.c_v, self.o_prev_v)
            for jj_i in range(4):
                jj = 4 * g4 + jj_i
                self.op("v", lambda e, jj=jj, ji=jj_i: e.tensor_copy(
                    self.sumT[:, 1024 * jj: 1024 * jj + 1024],
                    PS[:, 1024 * ji: 1024 * ji + 1024]), self.c_v)
            op_last_v = self.c_v.n
        self.a_outproj_mm = self.c_mm.n
        self.a_outproj_v = self.c_v.n

    def phase_M(self, b):
        PS = self.PS
        i = self.i
        self.wait("v", self.c_mm, self.f_mm_end)
        self.wait("v", self.c_v, self.f_v_end)
        for j in range(NTOKM):
            self.op("v", lambda e, j=j: e.tensor_add(
                self.keep_t[:], self.rank_sb[:, 16 * j: 16 * j + 16], self.offb[:]),
                self.c_v)
            self.wait("v", self.c_v)
            self.op("v", lambda e: e.tensor_scalar(
                out=self.keep_t[:], in0=self.keep_t[:], scalar1=CAP, scalar2=None,
                op0=ALU.is_le), self.c_v)
            self.wait("v", self.c_v)
            self.op("v", lambda e, j=j: e.tensor_mul(
                self.m1_sb[:], self.oh1_sb[:, E * j: E * j + E], self.keep_t[:, 0:8]),
                self.c_v)
            self.op("v", lambda e, j=j: e.tensor_mul(
                self.c1_sb[:], self.oh2_sb[:, E * j: E * j + E],
                self.keep_t[:, 8:16]), self.c_v)
            self.wait("v", self.c_v)
            self.op("v", lambda e, j=j: e.tensor_scalar(
                out=self.m1_sb[:], in0=self.m1_sb[:], scalar1=self.w1c[:, j: j + 1],
                scalar2=None, op0=ALU.mult), self.c_v)
            self.op("v", lambda e, j=j: e.tensor_scalar(
                out=self.c1_sb[:], in0=self.c1_sb[:], scalar1=self.w2c[:, j: j + 1],
                scalar2=None, op0=ALU.mult), self.c_v)
            self.wait("v", self.c_v)
            self.op("v", lambda e, j=j: e.tensor_add(
                self.w8_sb[:, E * j: E * j + E], self.m1_sb[:], self.c1_sb[:]),
                self.c_v)
            self.wait("v", self.c_v)
        w8_v = self.c_v.n

        w1bufs = [self.w1blk, self.w1blk2]
        w2bufs = [self.w2blk, self.w2blk2]
        w1_free = [0, 0]
        w2_free = [0, 0]
        b2_mm_free = 0
        y_ev = 0
        h_bank_s = {}
        for e_ in range(E):
            self.wait("g", self.c_mm, b2_mm_free)
            self.dma("g", self.b2_r[0:1, :], i["b2"][e_], self.d_b2)
            b2_dma = self.d_b2.n
            self.wait("t", self.d_cv[e_], 32)
            for c_ in range(NCH):
                for fb16 in range(NFT // 2):
                    wb = w1bufs[fb16 % 2]
                    self.wait("y", self.c_mm, w1_free[fb16 % 2])
                    self.dma("y", wb[:],
                             self.ap(self.scr["W1s"], e_ * D * FF + 256 * fb16,
                                     [[FF, 128], [128 * FF, NDT], [1, 256]]),
                             self.d_w1[fb16 % 2])
                    self.wait("t", self.d_w1[fb16 % 2])
                    if e_ == 0 and c_ == 0 and fb16 == 0:
                        self.wait("t", self.c_mm, self.a_outproj_mm)
                        self.wait("s", self.c_mm, self.a_outproj_mm)
                        self.wait("t", self.c_v, self.a_outproj_v)
                    for f2 in range(2):
                        ft = 2 * fb16 + f2
                        bk = ft % 4
                        gd = h_bank_s.get(bk, 0)
                        if gd:
                            self.wait("t", self.c_s, gd)
                        if ft == 0 and y_ev:
                            self.wait("t", self.c_v, y_ev)
                        for t_ in range(NDT):
                            self.mm(PS[:, 512 * bk: 512 * bk + 512],
                                    wb[:, 256 * t_ + 128 * f2:
                                       256 * t_ + 128 * f2 + 128],
                                    self.xnT[:, S * t_ + 512 * c_:
                                             S * t_ + 512 * c_ + 512],
                                    t_ == 0, t_ == NDT - 1,
                                    sem=(self.c_mm if t_ == NDT - 1 else None))
                        hm = self.c_mm.n
                        self.wait("s", self.c_mm, hm)
                        self.op("s", lambda e, ft=ft, ee=e_, bk=bk: e.activation(
                            self.AH[:, 512 * ft: 512 * ft + 512],
                            PS[:, 512 * bk: 512 * bk + 512], AF.Gelu,
                            bias=self.b1c[:, 32 * ee + ft: 32 * ee + ft + 1]),
                            self.c_s)
                        h_bank_s[bk] = self.c_s.n
                    w1_free[fb16 % 2] = self.c_mm.n
                h_s = self.c_s.n
                self.m_h_mm_end = self.c_mm.n
                self.wait("t", self.c_s, h_s)
                for ft in range(NFT):
                    wb2 = w2bufs[ft % 2]
                    self.wait("y", self.c_mm, w2_free[ft % 2])
                    self.dma("y", wb2[:],
                             self.ap(self.scr["W2s"], e_ * FF * D + 128 * ft * D,
                                     [[D, 128], [1, D]]), self.d_w2[ft % 2])
                    self.wait("t", self.d_w2[ft % 2])
                    for tm in range(4):
                        for half in range(2):
                            bk = 2 * tm + half
                            self.mm(PS[:, 512 * bk: 512 * bk + 512],
                                    self.AH[:, 512 * ft + 128 * tm:
                                             512 * ft + 128 * tm + 128],
                                    wb2[:, 512 * half: 512 * half + 512],
                                    ft == 0, False,
                                    sem=(self.c_mm if (tm == 3 and half == 1)
                                         else None))
                    w2_free[ft % 2] = self.c_mm.n
                    if ft == 0:
                        self.wait("t", self.d_b2, b2_dma)
                        for tm in range(4):
                            for half in range(2):
                                bk = 2 * tm + half
                                self.mm(PS[:, 512 * bk: 512 * bk + 512],
                                        self.ONESB[0:1, 0:128],
                                        self.b2_r[0:1, 512 * half: 512 * half + 512],
                                        False, False,
                                        sem=(self.c_mm if (tm == 3 and half == 1)
                                             else None))
                        w2_free[ft % 2] = self.c_mm.n
                y_mm = self.c_mm.n
                self.wait("v", self.c_mm, y_mm)
                self.wait("v", self.c_v, w8_v)
                self.wait("v", self.c_v)
                for tm in range(4):
                    j = 4 * c_ + tm
                    for half in range(2):
                        bk = 2 * tm + half
                        self.op("v", lambda e, j=j, h2=half, bk=bk, ee=e_:
                                e.scalar_tensor_tensor(
                                    out=self.sumT[:, 1024 * j + 512 * h2:
                                                  1024 * j + 512 * h2 + 512],
                                    in0=PS[:, 512 * bk: 512 * bk + 512],
                                    scalar=self.w8_sb[:, E * j + ee: E * j + ee + 1],
                                    in1=self.sumT[:, 1024 * j + 512 * h2:
                                                  1024 * j + 512 * h2 + 512],
                                    op0=ALU.mult, op1=ALU.add),
                                self.c_v if (tm == 3 and half == 1) else None)
                y_ev = self.c_v.n
                self.c_ev.inc()
            b2_mm_free = self.c_mm.n
        self.m_y_mm_end = self.c_mm.n
        self.m_done_v = self.c_v.n
        self.m_y_ev = y_ev

    def phase_O(self, b):
        i = self.i
        bufs = [self.xo_t, self.xo_t2]
        free_st = self.o_free_st
        self.wait("v", self.c_v, self.m_done_v)
        for j in range(NTOKM):
            sl = j % 2
            bu = bufs[sl]
            self.wait("y", self.d_st[sl], free_st[sl])
            self.dma("y", bu[:], i["x_own"][b, 128 * j: 128 * j + 128, :],
                     self.d_xo[sl])
            self.wait("v", self.d_xo[sl])
            self.op("v", lambda e, bu=bu, j=j: e.tensor_add(
                bu[:], bu[:], self.sumT[:, 1024 * j: 1024 * j + 1024]), self.c_v)
            self.wait("s", self.c_v)
            self.dma("s", self.y_out[b, 128 * j: 128 * j + 128, :], bu[:],
                     self.d_st[sl])
            free_st[sl] = self.d_st[sl].n
        self.o_prev_v = self.c_v.n

    def tap_dump(self, name, src_ap):
        self.wait("s", self.c_v)
        self.wait("s", self.c_s)
        self.dma("s", self.taps[name], src_ap, self.d_tap)

    def finish(self):
        self.wait("y", self.d_st0, self.d_st0.n)
        self.wait("y", self.d_st1, self.d_st1.n)
        self.wait("y", self.d_tap, self.d_tap.n)
        nc = self.nc
        q = self.q
        with nc.Block() as block:
            @block.tensor
            def _(e):
                for f in q["t"]:
                    f(e)

            @block.vector
            def _(e):
                for f in q["v"]:
                    f(e)

            @block.scalar
            def _(e):
                for f in q["s"]:
                    f(e)

            @block.gpsimd
            def _(e):
                for f in q["g"]:
                    f(e)

            @block.sync
            def _(e):
                for f in q["y"]:
                    f(e)
        self.es.__exit__(None, None, None)
        return nc


def build_full(nb=4, nfb=28, taps=(), upto=None):
    B = Builder(nb=nb, nfb=nfb, taps=taps)
    B.phase_W()
    B.phase_F()
    for b in range(nb):
        B.phase_R(b)
        if upto == "R":
            break
        B.phase_A(b)
        if upto == "A":
            break
        B.phase_M(b)
        if upto == "M":
            break
        B.phase_O(b)
    return B


def make_consts():
    import ml_dtypes
    inv = 10000.0 ** (-np.arange(0, HD, 2, dtype=np.float64) / HD)
    pos = np.arange(S, dtype=np.float64)
    fr = pos[:, None] * inv[None, :]
    cos32 = np.cos(fr).astype(np.float32)
    sin32 = np.sin(fr).astype(np.float32)
    r = np.arange(128) % 32
    cosrep = np.ascontiguousarray(cos32[:, r].T)
    sinrep = np.ascontiguousarray(sin32[:, r].T)
    k = np.arange(128)
    ltri = (k[:, None] <= k[None, :]).astype(np.float32)
    return {
        "cosrep": cosrep,
        "sinrep": sinrep,
        "ltri_b": ltri.astype(ml_dtypes.bfloat16),
        "ltri_f": ltri,
    }


def make_in_maps(x, weights, nb=4, nfb=28):
    xT = np.ascontiguousarray(x.transpose(0, 2, 1))
    shared = dict(weights)
    shared.update(make_consts())
    in_maps = []
    for c in range(NCORES):
        own = list(range(nb * c, nb * c + nb)) if x.shape[0] == NCORES * nb \
            else list(range(nb))
        forn = [bb for bb in range(x.shape[0]) if bb // nb != c] if nfb else []
        im = dict(shared)
        im["x_own"] = np.ascontiguousarray(x[own])
        im["xT_own"] = np.ascontiguousarray(xT[own])
        if nfb:
            im["xT_for"] = np.ascontiguousarray(xT[forn])
            im["m28"] = np.array([[1.0 if fb < nb * c else 0.0 for fb in forn]],
                                 np.float32)
        else:
            im["m28"] = np.zeros((1, 1), np.float32)
        in_maps.append(im)
    return in_maps


def kernel(L0_attn_x, L0_attn_wq, L0_attn_bq, L0_attn_wk, L0_attn_bk,
           L0_attn_wv, L0_attn_bv, L0_attn_wo, L0_attn_bo,
           L0_ffn_Wr, L0_ffn_W1, L0_ffn_b1, L0_ffn_W2, L0_ffn_b2):
    x = np.ascontiguousarray(np.asarray(L0_attn_x, dtype=np.float32))
    weights = {
        "wq": np.ascontiguousarray(L0_attn_wq, np.float32),
        "bq": np.ascontiguousarray(L0_attn_bq, np.float32).reshape(1, D),
        "wk": np.ascontiguousarray(L0_attn_wk, np.float32),
        "bk": np.ascontiguousarray(L0_attn_bk, np.float32).reshape(1, D),
        "wv": np.ascontiguousarray(L0_attn_wv, np.float32),
        "bv": np.ascontiguousarray(L0_attn_bv, np.float32).reshape(1, D),
        "wo": np.ascontiguousarray(L0_attn_wo, np.float32),
        "bo": np.ascontiguousarray(L0_attn_bo, np.float32).reshape(1, D),
        "Wr": np.ascontiguousarray(L0_ffn_Wr, np.float32),
        "W1": np.ascontiguousarray(L0_ffn_W1, np.float32),
        "b1": np.ascontiguousarray(L0_ffn_b1, np.float32),
        "W2": np.ascontiguousarray(L0_ffn_W2, np.float32),
        "b2": np.ascontiguousarray(L0_ffn_b2, np.float32),
    }
    B = build_full(nb=4, nfb=28)
    nc = B.finish()
    in_maps = make_in_maps(x, weights, nb=4, nfb=28)
    res = run_bass_kernel_spmd(nc, in_maps, core_ids=list(range(NCORES)))
    out = np.empty((32, S, D), np.float32)
    for c in range(NCORES):
        out[4 * c: 4 * c + 4] = res.results[c]["y_out"]
    return out
